# version 17 (final)
# baseline (speedup 1.0000x reference)
import math
import sys
import numpy as np
from contextlib import ExitStack

sys.path.insert(0, "/opt/trn_rl_repo")
import concourse.bass as bass
import concourse.mybir as mybir
from concourse.bass_utils import run_bass_kernel_spmd

F32 = mybir.dt.float32
BF16 = mybir.dt.bfloat16
U32 = mybir.dt.uint32
AF = mybir.ActivationFunctionType
ALU = mybir.AluOpType
AX = mybir.AxisListType.X

T, D, NK, DI = 2048, 1024, 8, 2048
CONVD, NJC, DTW = 2304, 18, 2336
H, CH, NCH = 32, 128, 16
EPS = 1e-5

SEM_NAMES = ["cD", "x0D", "wmA", "wmB", "wmC", "wmD", "mcst", "xD", "qD", "mmod", "mev", "sqv", "sqm",
             "stm", "rowv", "rows", "qmm", "qev", "hv", "st", "misc",
             "wzA", "wzB", "zmm", "zev", "dtv", "dts", "dtm", "cvm", "cvv", "cvs", "x2D",
             "g1", "g2", "g3", "g4", "g5", "g6", "g7", "g8", "g9", "g10", "g12", "g13",
             "h1s", "h2s", "h3s", "h4s", "h5s", "wA", "wB", "gtD", "gt2", "gwD",
             "gw2", "gw3", "gw4", "wS", "g2s", "eAD", "eBD", "oD", "oE"]



class EG:
    OPNAMES = {"tensor_tensor", "tensor_add", "tensor_sub", "tensor_mul", "tensor_max",
               "tensor_copy", "tensor_scalar", "tensor_single_scalar", "tensor_scalar_add",
               "tensor_scalar_sub", "tensor_scalar_mul", "tensor_scalar_max", "tensor_scalar_min",
               "scalar_tensor_tensor", "tensor_tensor_scan", "tensor_reduce", "reduce_sum",
               "reduce_max", "reciprocal", "memset", "memzero", "select", "max", "max_index",
               "iota", "activation", "copy", "add", "mul", "square", "sqrt", "sign",
               "transpose", "stream_shuffle"}

    def __init__(self, eng):
        self.e = eng
        self.w = []
        self.r = []

    @staticmethod
    def _span(ap):
        if not isinstance(ap, bass.AP):
            return None
        off = ap.offset
        if not isinstance(off, int):
            return (ap.tensor.name, -10**15, 10**15)
        ext = 1
        for st, ct in ap.ap:
            ext += abs(int(st)) * (int(ct) - 1)
        es = mybir.dt.size(ap.dtype)
        return (ap.tensor.name, off * es, (off + ext) * es)

    @staticmethod
    def _overlap(a, b):
        return a[0] == b[0] and a[1] < b[2] and b[1] < a[2]

    def drain(self):
        ins = self.e.drain()
        self.w = []
        self.r = []
        return ins

    def wait_ge(self, sem, val):
        return self.e.wait_ge(sem, val)

    def dma_start(self, *a, **kw):
        return self.e.dma_start(*a, **kw)

    def __getattr__(self, name):
        fn = getattr(self.e, name)
        if name not in EG.OPNAMES:
            return fn

        def wrapped(*args, **kwargs):
            aps = [a for a in args if isinstance(a, bass.AP)]
            if "accum_out" in kwargs and isinstance(kwargs["accum_out"], bass.AP):
                aps.append(kwargs["accum_out"])
            for k in ("bias", "scale"):
                if isinstance(kwargs.get(k), bass.AP):
                    aps.append(kwargs[k])
            outs = [aps[0]] if aps else []
            if "accum_out" in kwargs and isinstance(kwargs["accum_out"], bass.AP):
                outs.append(kwargs["accum_out"])
            ins = aps[1:]
            ospans = [s for s in (EG._span(a) for a in outs) if s]
            ispans = [s for s in (EG._span(a) for a in ins) if s]
            need = False
            for s in ispans + ospans:
                for wsp in self.w:
                    if EG._overlap(s, wsp):
                        need = True
            for s in ospans:
                for rsp in self.r:
                    if EG._overlap(s, rsp):
                        need = True
            if need:
                self.e.drain()
                self.w = []
                self.r = []
            instr = fn(*args, **kwargs)
            self.w.extend(ospans)
            self.r.extend(ispans)
            return instr

        return wrapped


class Sems:
    def __init__(self, nc, ctx, names):
        self.h = {n: ctx.enter_context(nc.semaphore(n)) for n in names}
        self.c = {n: 0 for n in names}

    def inc(self, instr, n, k=1):
        instr.then_inc(self.h[n], k)
        self.c[n] += k
        return instr

    def snap(self, n):
        return self.c[n]


def build_core_program(debug=()):
    nc = bass.Bass("TRN2", target_bir_lowering=False)
    dbg = set(debug)

    dram = lambda n, s, dt=F32: nc.dram_tensor(n, s, dt, kind="ExternalInput").ap()
    i_xT = dram("xT", [D, T])
    i_wqT = dram("wqT", [D, D])
    i_npw = dram("npw", [128, NK])
    i_ccol = dram("ccol", [128, NK])
    i_wmod = dram("wmod", [D, 6 * D])
    i_identf = dram("identf", [128, 128])
    i_iota32 = dram("iota32", [H, 1])
    i_wzxcT = dram("wzxcT", [D, DTW])
    i_convw = dram("convw", [128, NJC * 4])
    i_convb = dram("convb", [128, NJC])
    i_dtb = dram("dtb", [H, 1])
    i_alog = dram("alog", [H, 1])
    i_drep = dram("drep", [128, 16])
    i_dtm32 = dram("dtm32", [128, 32])
    i_tril = dram("trilneg", [128, 128])
    i_wzzT = dram("wzzT", [D, DI])
    i_woutT = dram("woutT", [DI, D])
    i_ngw = dram("ngw", [128, 16])
    i_wgate = dram("wgate", [D, 8])
    i_bgate = dram("bgate", [1, 8])
    i_alpha = dram("alpha", [1, 1])
    i_w1m = dram("w1m", [D, 4096])
    i_b1m = dram("b1m", [128, 32])
    i_w2m = dram("w2m", [4096, D])
    i_b2m = dram("b2m", [128, 8])
    i_w1e = dram("w1e", [8 * D, 4096])
    i_b1e = dram("b1e", [8 * 128, 32])
    i_w2e = dram("w2e", [8 * 4096, D])
    i_b2e = dram("b2e", [8 * 128, 8])
    o_out = nc.dram_tensor("o_xT", [D, T], F32, kind="ExternalOutput").ap()
    d_rows = nc.dram_tensor("d_rows", [1, T], BF16, kind="Internal").ap()
    d_w1b = nc.dram_tensor("d_w1b", [2 * D, 4096], BF16, kind="Internal").ap()
    d_w2b = nc.dram_tensor("d_w2b", [2 * 4096, D], BF16, kind="Internal").ap()

    dbg_outs = {}
    def dbg_out(name, shape, dt):
        if name in dbg:
            dbg_outs[name] = nc.dram_tensor("dbg_" + name, shape, dt, kind="ExternalOutput").ap()
        return dbg_outs.get(name)

    db_mod = dbg_out("modcols", [128, 48], F32)
    db_q = dbg_out("q", [128, NK * T], BF16)
    db_hin = dbg_out("hin", [128, NK * T], BF16)
    db_cos = dbg_out("cos", [128, T], BF16)
    db_sin = dbg_out("sin", [128, T], BF16)
    db_xbcc = dbg_out("xbcc", [128, NJC * T], BF16)
    db_dt = dbg_out("dt", [32, T], F32)
    db_acs = dbg_out("acs", [32, T], F32)
    db_tm = dbg_out("tmdt", [128, NCH * 64], BF16)
    db_y = dbg_out("yfm", [128, 16 * T], BF16)
    db_x1 = dbg_out("x1", [128, NK * T], BF16)

    sb = nc.alloc_sbuf_tensor
    x0b = sb("x0b", [128, NK * T], BF16)
    qb = sb("qb", [128, NK * T], BF16)
    xbc = sb("xbc", [128, NJC * T], BF16)
    hbuf = sb("hbuf", [128, NK * T], BF16)
    warena = sb("warena", [128, 4608], BF16)
    id_f = sb("id_f", [128, 128], F32)
    id_b = sb("id_b", [128, 128], BF16)
    tril_b = sb("tril_b", [128, 128], BF16)
    onesb = sb("onesb", [1, 128], BF16)
    onescol = sb("onescol", [128, 1], BF16)
    smallf = sb("smallf", [128, 384], F32)
    smallb = sb("smallb", [128, 128], BF16)
    S_sb = sb("S_sb", [128, DI], BF16)
    tmdt = sb("tmdt", [128, NCH * 64], BF16)
    negacs = sb("negacs", [128, NCH * 32], F32)
    expacs = sb("expacs", [128, NCH * 32], F32)
    acs_hi = sb("acs_hi", [32, T], BF16)
    acs_lo = sb("acs_lo", [32, T], BF16)
    convw_sb = sb("convw_sb", [128, NJC * 4], F32)
    rowtf = sb("rowtf", [1, 32], F32)
    rowtb = sb("rowtb", [1, 16], BF16)
    idx8 = sb("idx8", [1, 8], U32)
    top8 = sb("top8", [1, 8], F32)
    wg_sb = sb("wg_sb", [128, 64], BF16)

    COL = {}
    _o = [0]
    for nm, n in [("mod", 48), ("npw", NK), ("onepca", NK), ("ga", NK),
                  ("onepcb", NK), ("gb", NK),
                  ("eps", 1), ("pio2", 1), ("fcol", 1), ("cj", 1), ("sj", 1),
                  ("t5", 8), ("t6", 8), ("t7", 8), ("iota32", 1),
                  ("convb", NJC), ("dtb", 1), ("nega", 1), ("neglast", NCH),
                  ("expal", NCH), ("drep", 16), ("ngw", 16), ("b1m", 32), ("b2m", 8),
                  ("b1e1", 32), ("b1e2", 32), ("b2e1", 8), ("b2e2", 8), ("bcomb", 8),
                  ("coefm", 1), ("coefe1", 1), ("coefe2", 1), ("meancol", 8)]:
        COL[nm] = _o[0]
        _o[0] += n
    assert _o[0] <= 384, _o[0]

    def c_(name, i=0, n=1, p=128):
        b = COL[name] + i
        return smallf[0:p, b:b + n]

    CB = {}
    _o[0] = 0
    for nm, n in [("ccol", NK), ("scol", NK), ("mean8", NK), ("dtm32", 32)]:
        CB[nm] = _o[0]
        _o[0] += n

    def cb_(name, i=0, n=1):
        b = CB[name] + i
        return smallb[:, b:b + n]

    cosrep = warena[:, 0:2048]
    sinrep2 = warena[:, 2048:4096]

    def wq_slab(k):
        return xbc[:, k * 1024:(k + 1) * 1024]

    mu_row = xbc[0:1, 8192:10240]
    rstd_row = xbc[0:1, 10240:12288]
    rstd2_row = xbc[0:1, 12288:14336]
    scr0 = xbc[:, 14336:14848]
    scr1 = xbc[:, 14848:15360]
    qsw = xbc[:, 15360:17408]

    def wmod_set(par, k):
        base = 17408 + par * 2048 + k * 256
        return xbc[:, base:base + 256]

    fp32scr = xbc[:, 21504:33792].bitcast(F32)
    mod_row = fp32scr[0:1, 0:6144]
    cos_f = fp32scr[0:32, 0:2048]
    sin_f = fp32scr[0:32, 2048:4096]
    tsc1 = fp32scr[0:32, 4096:5120]
    tsc2 = fp32scr[0:32, 5120:6144]
    rowscr = xbc[:, 33792:36864].bitcast(F32)
    r_mu = rowscr[0:1, 0:512]
    r_msq = rowscr[0:1, 512:1024]
    r_var = rowscr[0:1, 1024:1536]

    def wz_slab(par, k):
        base = par * 2048 + k * 256
        return warena[:, base:base + 256]

    dtraw = S_sb[0:32, 0:T]
    h32 = hbuf[:, :].bitcast(F32)
    dt_fm = h32[0:32, 0:2048]
    acs_fm = h32[32:64, 0:2048]
    dec_fm = h32[64:96, 0:2048]
    adt_fm = h32[0:32, 2048:4096]
    dps1 = h32[0:32, 4096:6144]
    dps2 = h32[0:32, 6144:8192]
    stack3 = h32[0:96, 0:2048]
    convdiag = [hbuf[:, 12288 + i * 128: 12288 + (i + 1) * 128] for i in range(4)]
    halo = [hbuf[:, 12800:12803], hbuf[:, 12803:12806]]
    drow_stage = hbuf[0:16, 8960:8992]
    drow = hbuf[0:1, 8448:8960]
    x_tm = hbuf[:, 0:2048]
    xdtd_t = hbuf[:, 2048:4096]
    Y_tm = hbuf[:, 4096:6144]
    expss = [hbuf[:, 6144:6656], hbuf[:, 6656:7168]]
    Mh = [hbuf[:, 12288 + i * 512: 12288 + (i + 1) * 512] for i in range(4)]
    cbt = hbuf[:, 8192:8320]
    Br_tm = hbuf[:, 8320:8448]
    xdt_tm = hbuf[:, 8960:11008]
    tmp_tm = hbuf[:, 14336:16384]

    ctx = ExitStack()
    sm = Sems(nc, ctx, SEM_NAMES)
    snap = {}

    with ExitStack() as pst, nc.Block() as blk:
        p_mod = [pst.enter_context(nc.psum_tensor(f"p_mod{i}", [1, 256], F32)) for i in range(2)]
        p_modT = pst.enter_context(nc.psum_tensor("p_modT", [128, 48], F32))
        p_sum = [pst.enter_context(nc.psum_tensor(f"p_sum{i}", [1, 512], F32)) for i in range(2)]
        p_sq = [pst.enter_context(nc.psum_tensor(f"p_sq{i}", [1, 512], F32)) for i in range(2)]

        @blk.gpsimd
        def _(g):
            g = EG(g)
            sm.inc(g.dma_start(id_b[:, :], i_identf), "cD", 16)
            sm.inc(g.dma_start(tril_b[:, :], i_tril), "cD", 16)
            sm.inc(g.dma_start(cb_("ccol", 0, NK), i_ccol), "cD", 16)
            for k in range(NK):
                sm.inc(g.dma_start(x0b[:, k * T:(k + 1) * T],
                                   i_xT[k * 128:(k + 1) * 128, :]), "x0D", 16)
            sm.inc(g.dma_start(xbc[:, 0:8192].rearrange("p (k c) -> p k c", k=NK),
                               bass.AP(i_wqT.tensor, 0,
                                       [[1024, 128], [128 * 1024, NK], [1, 1024]])),
                   "qD", 16)

        @blk.sync
        def _(sy):
            sm.inc(sy.dma_start(id_f[:, :], i_identf), "xD", 16)
            sm.inc(sy.dma_start(c_("npw", 0, NK), i_npw), "xD", 16)
            sm.inc(sy.dma_start(c_("iota32", p=H), i_iota32), "xD", 16)
            sm.inc(sy.dma_start(c_("ngw", 0, 16), i_ngw), "xD", 16)
            snap["xD_consts"] = sm.snap("xD")
            qb32 = qb[:, :].bitcast(F32)
            wsem = ["wmA", "wmB", "wmC", "wmD"]
            for jg in range(24):
                if jg >= 4:
                    sy.wait_ge(sm.h["mcst"], jg - 3)
                s4 = jg % 4
                dst = qb32[:, s4 * 2048:(s4 + 1) * 2048].rearrange("p (k c) -> p k c", k=NK)
                sm.inc(sy.dma_start(dst,
                                    bass.AP(i_wmod.tensor, jg * 256,
                                            [[6144, 128], [128 * 6144, NK], [1, 256]])),
                       wsem[s4], 16)

        @blk.vector
        def _(v):
            v = EG(v)
            v.memset(c_("eps"), EPS)
            v.memset(c_("pio2"), math.pi / 2)
            v.memset(onesb[:, :], 1.0)
            v.memset(onescol[:, :], 1.0)
            sm.inc(v.drain(), "misc", 1)
            v.wait_ge(sm.h["x0D"], 16 * NK)
            idx = 0
            for tt in range(4):
                for k in range(NK):
                    if idx >= 2:
                        v.wait_ge(sm.h["sqm"], idx - 1)
                    s_ = x0b[:, k * T + tt * 512: k * T + (tt + 1) * 512]
                    sm.inc(v.tensor_mul(scr0 if idx % 2 == 0 else scr1, s_, s_), "sqv", 1)
                    idx += 1
                v.wait_ge(sm.h["sqm"], 8 * (tt + 1))
                ps, pq = p_sum[tt % 2], p_sq[tt % 2]
                v.tensor_scalar(r_mu, ps[:, :], 1.0 / D, None, ALU.mult)
                v.tensor_copy(mu_row[:, tt * 512:(tt + 1) * 512], r_mu)
                v.tensor_scalar(r_msq, pq[:, :], 1.0 / D, None, ALU.mult)
                v.tensor_mul(r_var, r_mu, r_mu)
                v.tensor_sub(r_var, r_msq, r_var)
                sm.inc(v.drain(), "rowv", 1)
                v.wait_ge(sm.h["rows"], tt + 1)
                v.tensor_copy(rstd_row[:, tt * 512:(tt + 1) * 512], r_msq)
                sm.inc(v.drain(), "rowv", 1)
            v.wait_ge(sm.h["mmod"], 25)
            sm.inc(v.tensor_copy(c_("mod", 0, 48), p_modT[:, :]), "mev", 1)

        @blk.scalar
        def _(s):
            s = EG(s)
            s.wait_ge(sm.h["cD"], 48)
            sm.inc(s.activation(cb_("scol", 0, NK), cb_("ccol", 0, NK), AF.Silu),
                   "misc", 1)
            s.wait_ge(sm.h["xD"], snap["xD_consts"])
            sm.inc(s.activation(c_("fcol", p=H), c_("iota32", p=H), AF.Exp,
                                scale=-math.log(10000.0) / 32.0), "misc", 1)
            qb32s = qb[:, :].bitcast(F32)
            wsem = ["wmA", "wmB", "wmC", "wmD"]
            for jg in range(24):
                s4 = jg % 4
                s.wait_ge(sm.h[wsem[s4]], 16 * (jg // 4 + 1))
                if jg >= 2:
                    s.wait_ge(sm.h["mmod"], jg - 1)
                sm.inc(s.activation(xbc[:, 17408 + (jg % 2) * 2048: 17408 + (jg % 2) * 2048 + 2048],
                                    qb32s[:, s4 * 2048:(s4 + 1) * 2048], AF.Copy),
                       "mcst", 1)
                if jg >= 2:
                    s.wait_ge(sm.h["mmod"], jg - 1)
                    sm.inc(s.copy(mod_row[:, (jg - 2) * 256:(jg - 1) * 256],
                                  p_mod[jg % 2][:, :]), "mev", 1)
            for jg in (22, 23):
                s.wait_ge(sm.h["mmod"], jg + 1)
                sm.inc(s.copy(mod_row[:, jg * 256:(jg + 1) * 256], p_mod[jg % 2][:, :]), "mev", 1)
            for tt in range(4):
                s.wait_ge(sm.h["rowv"], 2 * tt + 1)
                s.activation(r_msq, r_var, AF.Ln, bias=c_("eps", p=1))
                sm.inc(s.activation(r_msq, r_msq, AF.Exp, scale=-0.5), "rows", 1)

        @blk.tensor
        def _(t):
            t.wait_ge(sm.h["misc"], 2)
            for jg in range(24):
                t.wait_ge(sm.h["mcst"], jg + 1)
                if jg >= 2:
                    t.wait_ge(sm.h["mev"], jg - 1)
                for k in range(NK):
                    mm = t.matmul(p_mod[jg % 2][:, :], cb_("scol", k, 1), wmod_set(jg % 2, k),
                                  start=(k == 0), stop=(k == NK - 1))
                sm.inc(mm, "mmod", 1)
            t.wait_ge(sm.h["mev"], 24)
            t.wait_ge(sm.h["xD"], snap["xD_consts"])
            for i in range(48):
                t.transpose(p_modT[:, i:i + 1], mod_row[:, i * 128:(i + 1) * 128], id_f[0:1, 0:1])
            sm.inc(t.nop(), "mmod", 1)
            t.wait_ge(sm.h["x0D"], 16 * NK)
            idx = 0
            for tt in range(4):
                if tt >= 2:
                    t.wait_ge(sm.h["rowv"], 2 * (tt - 2) + 2)
                for k in range(NK):
                    t.matmul(p_sum[tt % 2][:, :], onescol[:, :],
                             x0b[:, k * T + tt * 512: k * T + (tt + 1) * 512],
                             start=(k == 0), stop=(k == NK - 1))
                    t.wait_ge(sm.h["sqv"], idx + 1)
                    mm = t.matmul(p_sq[tt % 2][:, :], onescol[:, :],
                                  scr0 if idx % 2 == 0 else scr1,
                                  start=(k == 0), stop=(k == NK - 1))
                    sm.inc(mm, "sqm", 1)
                    idx += 1

    with ExitStack() as pst, nc.Block() as blk:
        p_repA = [pst.enter_context(nc.psum_tensor(f"p_repA{i}", [128, 512], F32)) for i in range(2)]
        p_repB = [pst.enter_context(nc.psum_tensor(f"p_repB{i}", [128, 512], F32)) for i in range(2)]
        rowvB = sm.snap("rowv")
        rowsB = sm.snap("rows")
        qmmB = sm.snap("qmm")
        sqvB = sm.snap("sqv")
        hvB = sm.snap("hv")
        miscB = sm.snap("misc")

        @blk.vector
        def _(v):
            v = EG(v)
            v.tensor_scalar(c_("onepca", 0, NK), c_("mod", 8, NK), 1.0, None, ALU.add)
            v.tensor_scalar(c_("onepcb", 0, NK), c_("mod", 32, NK), 1.0, None, ALU.add)
            for gi, (srcc, dstn) in enumerate(((16, "ga"), (40, "gb"))):
                g8 = c_("mod", srcc, NK)
                v.tensor_scalar(c_("t5", 0, NK), g8, 0.0, None, ALU.is_ge)
                v.tensor_scalar(c_("t5", 0, NK), c_("t5", 0, NK), 2.0, -1.0, ALU.mult, ALU.add)
                v.wait_ge(sm.h["rows"], rowsB + 3 + gi)
                absb = c_("t6", 0, NK) if gi == 0 else c_("t7", 0, NK)
                v.tensor_scalar(absb, absb, 0.25, None, ALU.max)
                v.tensor_mul(c_(dstn, 0, NK), c_("t5", 0, NK), absb)
            v.drain()
            sm.inc(v.drain(), "misc", 1)
            v.memset(cos_f[:, 0:1], 1.0)
            v.memset(sin_f[:, 0:1], 0.0)
            v.wait_ge(sm.h["rows"], rowsB + 2)
            cj = c_("cj", p=32)
            sjc = c_("sj", p=32)
            for lev in range(11):
                w = 1 << lev
                v.tensor_scalar(tsc1[:, 0:w], cos_f[:, 0:w], cj, None, ALU.mult)
                v.tensor_scalar(tsc2[:, 0:w], sin_f[:, 0:w], sjc, None, ALU.mult)
                v.tensor_sub(cos_f[:, w:2 * w], tsc1[:, 0:w], tsc2[:, 0:w])
                v.tensor_scalar(tsc1[:, 0:w], sin_f[:, 0:w], cj, None, ALU.mult)
                v.tensor_scalar(tsc2[:, 0:w], cos_f[:, 0:w], sjc, None, ALU.mult)
                v.tensor_add(sin_f[:, w:2 * w], tsc1[:, 0:w], tsc2[:, 0:w])
                if lev < 10:
                    v.tensor_mul(c_("t5", p=32), cj, cj)
                    v.tensor_mul(c_("t6", p=32), sjc, sjc)
                    v.tensor_mul(c_("t7", p=32), cj, sjc)
                    v.tensor_sub(cj, c_("t5", p=32), c_("t6", p=32))
                    v.tensor_add(sjc, c_("t7", p=32), c_("t7", p=32))
                    v.drain()
            for r in range(4):
                v.tensor_copy(cosrep[r * 32:(r + 1) * 32, :], cos_f[:, :])
                if r % 2 == 0:
                    v.tensor_scalar(sinrep2[r * 32:(r + 1) * 32, :], sin_f[:, :], -1.0, None, ALU.mult)
                else:
                    v.tensor_copy(sinrep2[r * 32:(r + 1) * 32, :], sin_f[:, :])
            sm.inc(v.drain(), "misc", 1)
            for tt in range(4):
                v.wait_ge(sm.h["qmm"], qmmB + tt + 1)
                for k in range(NK):
                    xs = x0b[:, k * T + tt * 512: k * T + (tt + 1) * 512]
                    v.tensor_sub(scr0, xs, p_repA[tt % 2][:, :])
                    v.tensor_mul(scr0, scr0, p_repB[tt % 2][:, :])
                    v.tensor_scalar(hbuf[:, k * T + tt * 512: k * T + (tt + 1) * 512],
                                    scr0, c_("onepca", k), c_("mod", 0 + k),
                                    ALU.mult, ALU.add)
                    idx_unused = 0
                sm.inc(v.drain(), "rowv", 1)

        @blk.scalar
        def _(s):
            s = EG(s)
            sm.inc(s.activation(c_("cj", p=32), c_("fcol", p=32), AF.Sin,
                                scale=-1.0, bias=c_("pio2", p=32)), "rows", 1)
            sm.inc(s.activation(c_("sj", p=32), c_("fcol", p=32), AF.Sin), "rows", 1)
            sm.inc(s.activation(c_("t6", 0, NK), c_("mod", 16, NK), AF.Abs), "rows", 1)
            sm.inc(s.activation(c_("t7", 0, NK), c_("mod", 40, NK), AF.Abs), "rows", 1)

        @blk.tensor
        def _(t):
            for tt in range(4):
                if tt >= 2:
                    t.wait_ge(sm.h["rowv"], rowvB + (tt - 2) + 1)
                t.matmul(p_repA[tt % 2][:, :], onesb[:, :],
                         mu_row[:, tt * 512:(tt + 1) * 512], start=True, stop=True)
                sm.inc(t.matmul(p_repB[tt % 2][:, :], onesb[:, :],
                                rstd_row[:, tt * 512:(tt + 1) * 512], start=True, stop=True),
                       "qmm", 1)

    with ExitStack() as pst, nc.Block() as blk:
        p_q = [pst.enter_context(nc.psum_tensor(f"p_q{i}", [128, 512], F32)) for i in range(2)]
        p_s2 = [pst.enter_context(nc.psum_tensor(f"p_s2{i}", [1, 512], F32)) for i in range(2)]
        p_rep2 = [pst.enter_context(nc.psum_tensor(f"p_rep2{i}", [128, 512], F32)) for i in range(2)]
        qmm0 = sm.snap("qmm")
        qev0 = sm.snap("qev")
        sqv0 = sm.snap("sqv")
        sqm0 = sm.snap("sqm")
        stm0 = sm.snap("stm")
        rowv0 = sm.snap("rowv")
        rows0 = sm.snap("rows")
        hv0 = sm.snap("hv")

        @blk.tensor
        def _(t):
            t.wait_ge(sm.h["qD"], 16)
            idx = 0
            for j in range(NK):
                for tt in range(4):
                    if j == 0:
                        t.wait_ge(sm.h["rowv"], rowv0 - 3 + tt)
                    if idx >= 2:
                        t.wait_ge(sm.h["qev"], qev0 + idx - 1)
                    for k in range(NK):
                        mm = t.matmul(p_q[idx % 2][:, :],
                                      wq_slab(k)[:, j * 128:(j + 1) * 128],
                                      hbuf[:, k * T + tt * 512: k * T + (tt + 1) * 512],
                                      start=(k == 0), stop=(k == NK - 1))
                    sm.inc(mm, "qmm", 1)
                    idx += 1
            idx = 0
            for tt in range(4):
                if tt >= 2:
                    t.wait_ge(sm.h["rowv"], rowv0 + 2 * (tt - 2) + 1)
                for k in range(NK):
                    t.wait_ge(sm.h["sqv"], sqv0 + idx + 1)
                    mm = t.matmul(p_s2[tt % 2][:, :], onescol[:, :],
                                  scr0 if idx % 2 == 0 else scr1,
                                  start=(k == 0), stop=(k == NK - 1))
                    sm.inc(mm, "sqm", 1)
                    idx += 1
            for tt in range(4):
                t.wait_ge(sm.h["rowv"], rowv0 + 2 * tt + 2)
                if tt >= 2:
                    t.wait_ge(sm.h["sqv"], sqv0 + 32 + 8 * (tt - 2) + 8)
                sm.inc(t.matmul(p_rep2[tt % 2][:, :], onesb[:, :],
                                rstd2_row[:, tt * 512:(tt + 1) * 512], start=True, stop=True),
                       "qmm", 1)

        @blk.vector
        def _(v):
            v = EG(v)
            for idx in range(32):
                v.wait_ge(sm.h["qmm"], qmm0 + idx + 1)
                j, tt = divmod(idx, 4)
                sm.inc(v.tensor_copy(qb[:, j * T + tt * 512: j * T + (tt + 1) * 512],
                                     p_q[idx % 2][:, :]), "qev", 1)
            for j in range(NK):
                qs = qb[:, j * T:(j + 1) * T]
                v.tensor_copy(qsw[0:32, :], qs[32:64, :])
                v.tensor_copy(qsw[32:64, :], qs[0:32, :])
                v.tensor_copy(qsw[64:96, :], qs[96:128, :])
                v.tensor_copy(qsw[96:128, :], qs[64:96, :])
                v.tensor_mul(qsw[:, :], qsw[:, :], sinrep2[:, :])
                v.tensor_mul(qs[:, :], qs[:, :], cosrep[:, :])
                v.tensor_add(qs[:, :], qs[:, :], qsw[:, :])
            sm.inc(v.drain(), "misc", 1)
            idx = 0
            for tt in range(4):
                for k in range(NK):
                    if idx >= 2:
                        v.wait_ge(sm.h["sqm"], sqm0 + idx - 1)
                    qs = qb[:, k * T + tt * 512: k * T + (tt + 1) * 512]
                    sm.inc(v.tensor_mul(scr0 if idx % 2 == 0 else scr1, qs, qs), "sqv", 1)
                    idx += 1
                v.wait_ge(sm.h["sqm"], sqm0 + 8 * (tt + 1))
                v.tensor_scalar(r_msq, p_s2[tt % 2][:, :], 1.0 / D, EPS, ALU.mult, ALU.add)
                sm.inc(v.drain(), "rowv", 1)
                v.wait_ge(sm.h["rows"], rows0 + tt + 1)
                v.tensor_copy(rstd2_row[:, tt * 512:(tt + 1) * 512], r_var)
                sm.inc(v.drain(), "rowv", 1)
            idx = 0
            for tt in range(4):
                v.wait_ge(sm.h["qmm"], qmm0 + 32 + tt + 1)
                for k in range(NK):
                    if idx < 2:
                        v.wait_ge(sm.h["sqm"], sqm0 + 30 + idx + 1)
                    else:
                        v.wait_ge(sm.h["hv"], hv0 + idx - 1)
                    qs = qb[:, k * T + tt * 512: k * T + (tt + 1) * 512]
                    sm.inc(v.tensor_mul(scr0 if idx % 2 == 0 else scr1, qs,
                                        p_rep2[tt % 2][:, :]), "sqv", 1)
                    idx += 1

        @blk.scalar
        def _(s):
            s = EG(s)
            for tt in range(4):
                s.wait_ge(sm.h["rowv"], rowv0 + 2 * tt + 1)
                s.activation(r_var, r_msq, AF.Ln)
                sm.inc(s.activation(r_var, r_var, AF.Exp, scale=-0.5), "rows", 1)
            idx = 0
            for tt in range(4):
                for k in range(NK):
                    s.wait_ge(sm.h["sqv"], sqv0 + 32 + idx + 1)
                    sm.inc(s.activation(hbuf[:, k * T + tt * 512: k * T + (tt + 1) * 512],
                                        scr0 if idx % 2 == 0 else scr1, AF.Copy,
                                        scale=c_("npw", k)), "hv", 1)
                    idx += 1


        @blk.sync
        def _(sy):
            sy.wait_ge(sm.h["rowv"], rowv0 + 8)
            sm.inc(sy.dma_start(d_rows, rstd2_row), "st", 16)

    jlist = [(j, 128) for j in range(NJC)] + [(NJC, 32)]
    groups = [jlist[i:i + 2] for i in range(0, len(jlist), 2)]
    evcum = []
    tot = 0
    for grp in groups:
        tot += len(grp) * 4
        evcum.append(tot)
    with ExitStack() as pst, nc.Block() as blk:
        p_z = [pst.enter_context(nc.psum_tensor(f"p_z{i}", [128, 512], F32)) for i in range(8)]
        stD0 = sm.snap("st")

        @blk.gpsimd
        def _(g):
            g = EG(g)
            for gi, grp in enumerate(groups):
                if gi >= 2:
                    g.wait_ge(sm.h["zev"], evcum[gi - 2])
                par = gi % 2
                c0 = grp[0][0] * 128
                w = sum(x[1] for x in grp)
                for k in range(NK):
                    sm.inc(g.dma_start(wz_slab(par, k)[:, 0:w],
                                       i_wzxcT[k * 128:(k + 1) * 128, c0:c0 + w]),
                           "wzA" if par == 0 else "wzB", 16)
            sm.inc(g.dma_start(convw_sb[:, :], i_convw), "x2D", 16)
            sm.inc(g.dma_start(c_("convb", 0, NJC), i_convb), "x2D", 16)
            sm.inc(g.dma_start(c_("dtb", p=H), i_dtb), "x2D", 16)
            sm.inc(g.dma_start(c_("nega", p=H), i_alog), "x2D", 16)
            sm.inc(g.dma_start(c_("drep", 0, 16), i_drep), "x2D", 16)
            sm.inc(g.dma_start(cb_("dtm32", 0, 32), i_dtm32), "x2D", 16)

        @blk.tensor
        def _(t):
            for gi, grp in enumerate(groups):
                par = gi % 2
                t.wait_ge(sm.h["wzA" if par == 0 else "wzB"], 16 * NK * (gi // 2 + 1))
                if gi >= 1:
                    t.wait_ge(sm.h["zev"], evcum[gi - 1])
                for jj, (j, w) in enumerate(grp):
                    for tt in range(4):
                        bank = jj * 4 + tt
                        for k in range(NK):
                            mm = t.matmul(p_z[bank][0:w, :],
                                          wz_slab(par, k)[:, jj * 128: jj * 128 + w],
                                          hbuf[:, k * T + tt * 512:k * T + (tt + 1) * 512],
                                          start=(k == 0), stop=(k == NK - 1))
                        sm.inc(mm, "zmm", 1)

        @blk.vector
        def _(v):
            v = EG(v)
            v.wait_ge(sm.h["st"], stD0)
            nz = 0
            for gi, grp in enumerate(groups):
                for jj, (j, w) in enumerate(grp):
                    for tt in range(4):
                        bank = jj * 4 + tt
                        nz += 1
                        v.wait_ge(sm.h["zmm"], nz)
                        if j < NJC:
                            dst = xbc[:, j * T + tt * 512: j * T + (tt + 1) * 512]
                        else:
                            dst = dtraw[:, tt * 512:(tt + 1) * 512]
                        sm.inc(v.tensor_copy(dst, p_z[bank][0:w, :]), "zev", 1)

    with ExitStack() as pst, nc.Block() as blk:
        p_dtt = pst.enter_context(nc.psum_tensor("p_dtt", [128, 96], F32))
        p_drw = pst.enter_context(nc.psum_tensor("p_drw", [16, 32], F32))
        dtv0 = sm.snap("dtv")
        dts0 = sm.snap("dts")
        dtm0 = sm.snap("dtm")

        @blk.vector
        def _(v):
            v = EG(v)
            v.wait_ge(sm.h["x2D"], sm.c["x2D"])
            v.tensor_scalar(dps1, dtraw, c_("dtb", p=32), None, ALU.add)
            sm.inc(v.drain(), "dtv", 1)
            v.tensor_scalar(dps2, dps1, 0.0, None, ALU.max)
            v.wait_ge(sm.h["dts"], dts0 + 1)
            v.tensor_add(dt_fm, dt_fm, dps2)
            v.wait_ge(sm.h["dts"], dts0 + 2)
            v.tensor_scalar(c_("nega", p=32), c_("nega", p=32), -1.0, None, ALU.mult)
            v.drain()
            v.tensor_scalar(adt_fm, dt_fm, c_("nega", p=32), None, ALU.mult)
            v.drain()
            for cc in range(NCH):
                v.tensor_tensor_scan(acs_fm[:, cc * 128:(cc + 1) * 128],
                                     adt_fm[:, cc * 128:(cc + 1) * 128],
                                     adt_fm[:, cc * 128:(cc + 1) * 128],
                                     0.0, ALU.add, ALU.bypass)
            v.tensor_copy(smallf[32:64, COL["neglast"]:COL["neglast"] + NCH],
                          acs_fm[:, 127::128])
            sm.inc(v.drain(), "dtv", 1)
            v.tensor_copy(dps2, acs_fm)
            v.drain()
            v.tensor_copy(acs_hi[:, :], dps2)
            v.drain()
            v.tensor_tensor(acs_lo[:, :], dps2, acs_hi[:, :], ALU.subtract)
            v.wait_ge(sm.h["dts"], dts0 + 3 + NCH)
            sm.inc(v.drain(), "dtv", 1)
            for cc in range(NCH):
                v.wait_ge(sm.h["dtm"], dtm0 + cc + 1)
                v.tensor_copy(tmdt[:, cc * 64: cc * 64 + 32], p_dtt[:, 0:32])
                v.tensor_copy(tmdt[:, cc * 64 + 32: cc * 64 + 64], p_dtt[:, 64:96])
                sm.inc(v.tensor_scalar(negacs[:, cc * 32:(cc + 1) * 32], p_dtt[:, 32:64],
                                       -1.0, None, ALU.mult), "dtv", 1)
            v.wait_ge(sm.h["dtm"], dtm0 + NCH + 1)
            sm.inc(v.tensor_copy(drow_stage, p_drw[:, :]), "dtv", 1)

        @blk.scalar
        def _(s):
            s = EG(s)
            s.wait_ge(sm.h["dtv"], dtv0 + 1)
            s.activation(dt_fm, dps1, AF.Abs)
            s.activation(dt_fm, dt_fm, AF.Exp, scale=-1.0)
            sm.inc(s.activation(dt_fm, dt_fm, AF.Ln, bias=1.0), "dts", 1)
            s.wait_ge(sm.h["x2D"], sm.c["x2D"])
            sm.inc(s.activation(c_("nega", p=32), c_("nega", p=32), AF.Exp), "dts", 1)
            s.wait_ge(sm.h["dtv"], dtv0 + 2)
            sm.inc(s.activation(c_("expal", 0, NCH, p=32),
                                smallf[32:64, COL["neglast"]:COL["neglast"] + NCH],
                                AF.Exp), "dts", 1)
            for cc in range(NCH):
                sm.inc(s.activation(dec_fm[:, cc * 128:(cc + 1) * 128],
                                    acs_fm[:, cc * 128:(cc + 1) * 128],
                                    AF.Exp, scale=-1.0,
                                    bias=smallf[32:64, COL["neglast"] + cc:COL["neglast"] + cc + 1]),
                       "dts", 1)
            for cc in range(NCH):
                s.wait_ge(sm.h["dtv"], dtv0 + 3 + cc + 1)
                sm.inc(s.activation(expacs[:, cc * 32:(cc + 1) * 32],
                                    negacs[:, cc * 32:(cc + 1) * 32],
                                    AF.Exp, scale=-1.0), "dts", 1)

        @blk.tensor
        def _(t):
            t.wait_ge(sm.h["dtv"], dtv0 + 3)
            t.wait_ge(sm.h["dts"], dts0 + 3 + NCH)
            for cc in range(NCH):
                if cc >= 1:
                    t.wait_ge(sm.h["dtv"], dtv0 + 3 + cc)
                sm.inc(t.transpose(p_dtt[:, :], stack3[:, cc * 128:(cc + 1) * 128],
                                   id_f[0:96, 0:96]), "dtm", 1)
            t.wait_ge(sm.h["dts"], dts0 + 3)
            sm.inc(t.transpose(p_drw[:, :], c_("expal", 0, NCH, p=32), id_f[0:32, 0:32]),
                   "dtm", 1)

    with ExitStack() as pst, nc.Block() as blk:
        p_cv = [pst.enter_context(nc.psum_tensor(f"p_cv{i}", [128, 512], F32)) for i in range(2)]
        cvm0 = sm.snap("cvm")
        cvv0 = sm.snap("cvv")
        cvs0 = sm.snap("cvs")

        @blk.gpsimd
        def _(g):
            sm.inc(g.dma_start(drow, drow_stage), "x2D", 16)

        @blk.vector
        def _(v):
            v = EG(v)
            for j in range(NJC):
                if j >= 1:
                    v.wait_ge(sm.h["cvm"], cvm0 + 4 * j)
                for k in range(4):
                    v.tensor_scalar(convdiag[k], id_b[:, :],
                                    convw_sb[:, j * 4 + k:j * 4 + k + 1], None, ALU.mult)
                sm.inc(v.drain(), "cvv", 1)
                for tt in range(3):
                    v.wait_ge(sm.h["cvm"], cvm0 + 4 * j + tt)
                    sm.inc(v.tensor_copy(halo[tt % 2],
                                         xbc[:, j * T + (tt + 1) * 512 - 3: j * T + (tt + 1) * 512]),
                           "cvv", 1)

        @blk.tensor
        def _(t):
            idx = 0
            for j in range(NJC):
                for tt in range(4):
                    t.wait_ge(sm.h["cvv"], cvv0 + 4 * j + 1 + min(tt, 3))
                    if idx >= 2:
                        t.wait_ge(sm.h["cvs"], cvs0 + idx - 1)
                    bank = p_cv[idx % 2]
                    base = j * T + tt * 512
                    t.matmul(bank[:, :], convdiag[3], xbc[:, base: base + 512],
                             start=True, stop=False)
                    for k in range(3):
                        d = k - 3
                        if tt > 0:
                            t.matmul(bank[:, 0:-d], convdiag[k],
                                     halo[(tt - 1) % 2][:, 3 + d: 3],
                                     start=False, stop=False)
                        mm = t.matmul(bank[:, -d:512], convdiag[k],
                                      xbc[:, base: base + 512 + d],
                                      start=False, stop=(k == 2))
                    sm.inc(mm, "cvm", 1)
                    idx += 1

        @blk.scalar
        def _(s):
            s = EG(s)
            idx = 0
            for j in range(NJC):
                for tt in range(4):
                    idx += 1
                    s.wait_ge(sm.h["cvm"], cvm0 + idx)
                    s.wait_ge(sm.h["cvv"], cvv0 + 4 * j + 1 + min(tt + 1, 3))
                    bank = p_cv[(idx - 1) % 2]
                    sm.inc(s.activation(xbc[:, j * T + tt * 512: j * T + (tt + 1) * 512],
                                        bank[:, :], AF.Silu, bias=c_("convb", j)), "cvs", 1)

    with ExitStack() as pst, nc.Block() as blk:
        p_trx = [pst.enter_context(nc.psum_tensor(f"p_tr{i}", [128, 1024], BF16))
                 for i in range(2)]
        p_ss = pst.enter_context(nc.psum_tensor("p_ss", [128, 512], F32))
        p_yd = [pst.enter_context(nc.psum_tensor(f"p_yd{i}", [128, 512], F32)) for i in range(2)]
        p_yo = [pst.enter_context(nc.psum_tensor(f"p_yo{i}", [128, 512], F32)) for i in range(2)]
        p_st = p_yo[1][:, :]
        p_dr = p_yo[0][:, :]

        g0 = {n: sm.snap(n) for n in ("g1", "g2", "g2s", "g3", "g4", "g5", "g6", "g7",
                                      "g8", "g9", "g10", "g12", "g13")}
        zev0G = sm.snap("zev")
        dts_all = sm.c["dts"]
        x2D_all = sm.c["x2D"]
        ydiag_rec = {}
        zz = [0]

        def slotof(n):
            return p_trx[n % 2], 0

        @blk.tensor
        def _(t):
            t.wait_ge(sm.h["x2D"], x2D_all)
            for c in range(NCH):
                for k in range(5):
                    n = 9 * c + k
                    if k >= 2:
                        t.wait_ge(sm.h["g2s"], g0["g2s"] + 5 * c + k - 1)
                    elif c > 0:
                        t.wait_ge(sm.h["g2"], g0["g2"] + 4 * c - 1 + k)
                    bank, cb0 = slotof(n)
                    if k < 4:
                        for j in range(4):
                            mm = t.transpose(bank[:, cb0 + j * 128: cb0 + (j + 1) * 128],
                                             xbc[:, (4 * k + j) * T + c * 128:
                                                 (4 * k + j) * T + (c + 1) * 128],
                                             id_b[:, :])
                    else:
                        mm = t.transpose(bank[:, cb0: cb0 + 128],
                                         xbc[:, 16 * T + c * 128: 16 * T + (c + 1) * 128],
                                         id_b[:, :])
                    sm.inc(mm, "g1", 1)
                if c > 0:
                    t.wait_ge(sm.h["g10"], g0["g10"] + 8 * c)
                sm.inc(t.matmul(p_ss[:, 0:128],
                                xbc[:, 16 * T + c * 128: 16 * T + (c + 1) * 128],
                                xbc[:, 17 * T + c * 128: 17 * T + (c + 1) * 128],
                                start=True, stop=True), "g1", 1)
                for half in range(2):
                    hh0 = half * 16
                    if c > 0:
                        t.wait_ge(sm.h["g6"], g0["g6"] + 4 * c + 2 * half)
                        for a in range(2):
                            sm.inc(t.matmul(p_yo[a][:, :],
                                            xbc[:, 17 * T + c * 128: 17 * T + (c + 1) * 128],
                                            S_sb[:, (hh0 + 8 * a) * 64: (hh0 + 8 * a) * 64 + 512],
                                            start=True, stop=True), "g4", 1)
                            zz[0] += 1
                    for hg4 in range(4):
                        gidx = 8 * c + 4 * half + hg4
                        if gidx > 0:
                            t.wait_ge(sm.h["g10"], g0["g10"] + gidx)
                        t.wait_ge(sm.h["zev"], zev0G + c + 1)
                        for hi in range(4):
                            h = hh0 + hg4 * 4 + hi
                            sl = p_ss[:, hi * 128:(hi + 1) * 128]
                            lhsT_hot = id_b[0:32, h:h + 1].broadcast_to([32, 128])
                            t.matmul(sl, lhsT_hot, acs_hi[:, c * 128:(c + 1) * 128],
                                     start=True, stop=False)
                            t.matmul(sl, lhsT_hot, acs_lo[:, c * 128:(c + 1) * 128],
                                     start=False, stop=False)
                            mmm = t.matmul(sl, id_b[:, :], tril_b[:, :],
                                           start=False, stop=True)
                        sm.inc(mmm, "g3", 1)
                    t.wait_ge(sm.h["g9"], g0["g9"] + c + 1)
                    for a in range(2):
                        if half == 1:
                            t.wait_ge(sm.h["g7"], g0["g7"] + 6 * c + (a + 1))
                        elif c > 0:
                            t.wait_ge(sm.h["g7"], g0["g7"] + 6 * (c - 1) + 3 + (a + 1))
                        for hg4c in range(2):
                            grp = 4 * half + 2 * a + hg4c
                            g = 8 * c + grp
                            t.wait_ge(sm.h["g5"], g0["g5"] + g + 1)
                            for hi in range(4):
                                h = hh0 + 8 * a + 4 * hg4c + hi
                                mmd = t.matmul(p_yd[a][:, (h % 8) * 64:(h % 8) * 64 + 64],
                                               Mh[g % 4][:, (h % 4) * 128: (h % 4 + 1) * 128],
                                               xdt_tm[:, h * 64: h * 64 + 64],
                                               start=True, stop=True)
                            sm.inc(mmd, "g4", 1)
                            zz[0] += 1
                            ydiag_rec[g] = g0["g4"] + zz[0]
                    t.wait_ge(sm.h["g7"], g0["g7"] + 6 * c + 3 * half + 2)
                    t.wait_ge(sm.h["g2s"], g0["g2s"] + 5 * c + 5)
                    for a in range(2):
                        h0 = hh0 + 8 * a
                        ghg8 = 4 * c + 2 * half + a
                        if ghg8 > 0:
                            t.wait_ge(sm.h["g6"], g0["g6"] + ghg8)
                        sm.inc(t.matmul(p_st, Br_tm[:, :],
                                        xdtd_t[:, h0 * 64: h0 * 64 + 512],
                                        start=True, stop=True), "g12", 1)
                        dr_ap = bass.AP(drow.tensor, drow.offset + c * 32 + h0,
                                        [list(drow.ap[0]), [1, 8], [0, 64]])
                        sm.inc(t.matmul(p_dr, onesb[:, :], dr_ap,
                                        start=True, stop=True), "g13", 1)
                for b in range(4):
                    n = 9 * c + 5 + b
                    if b < 2:
                        t.wait_ge(sm.h["g2s"], g0["g2s"] + 5 * c + 4 + b)
                    else:
                        t.wait_ge(sm.h["g2"], g0["g2"] + 4 * c + b - 1)
                    t.wait_ge(sm.h["g7"], g0["g7"] + 6 * c + 3 * (1 if b < 2 else 2))
                    bank, cb0 = slotof(n)
                    for j in range(4):
                        mm = t.transpose(bank[:, cb0 + j * 128: cb0 + (j + 1) * 128],
                                         Y_tm[:, (4 * b + j) * 128:(4 * b + j + 1) * 128],
                                         id_b[:, :])
                    sm.inc(mm, "g8", 1)

        @blk.vector
        def _(v):
            v = EG(v)
            v.memset(S_sb[:, :], 0.0)
            v.wait_ge(sm.h["dts"], dts_all)
            v.wait_ge(sm.h["x2D"], x2D_all)
            for c in range(NCH):
                v.wait_ge(sm.h["g1"], g0["g1"] + 6 * c + 6)
                sm.inc(v.tensor_copy(cbt[:, :], p_ss[:, 0:128]), "zev", 1)
                v.wait_ge(sm.h["g2s"], g0["g2s"] + 5 * c + 4)
                tmap = tmdt[:, :]
                dt3 = bass.AP(tmap.tensor, tmap.offset + c * 64,
                              [list(tmap.ap[0]), [1, 32], [0, 64]])
                dec3 = bass.AP(tmap.tensor, tmap.offset + c * 64 + 32,
                               [list(tmap.ap[0]), [1, 32], [0, 64]])
                v.tensor_tensor(xdt_tm[:, :], x_tm[:, :], dt3, ALU.mult)
                v.tensor_tensor(xdtd_t[:, :], xdt_tm[:, :], dec3, ALU.mult)
                sm.inc(v.drain(), "g9", 1)
                cbt_b = bass.AP(cbt.tensor, cbt.offset, [list(cbt.ap[0]), [0, 4], [1, 128]])
                for half in range(2):
                    hh0 = half * 16
                    for g4i in range(4):
                        gg = 8 * c + 4 * half + g4i
                        v.wait_ge(sm.h["g10"], g0["g10"] + gg + 1)
                        if gg >= 4:
                            v.wait_ge(sm.h["g4"], ydiag_rec[gg - 4])
                        sm.inc(v.tensor_tensor(Mh[gg % 4][:, :], expss[gg % 2][:, :], cbt_b,
                                               ALU.mult), "g5", 1)
                    for a in range(2):
                        v.wait_ge(sm.h["g4"], ydiag_rec[8 * c + 4 * half + 2 * a + 1])
                        dst = Y_tm[:, (hh0 + 8 * a) * 64: (hh0 + 8 * a) * 64 + 512]
                        if c == 0:
                            sm.inc(v.tensor_copy(dst, p_yd[a][:, :]), "g7", 1)
                        else:
                            exap = expacs[:, :]
                            ex3 = bass.AP(exap.tensor,
                                          exap.offset + c * 32 + hh0 + 8 * a,
                                          [list(exap.ap[0]), [1, 8], [0, 64]])
                            v.tensor_tensor(dst, p_yo[a][:, :], ex3, ALU.mult)
                            sm.inc(v.tensor_tensor(dst, dst, p_yd[a][:, :], ALU.add),
                                   "g7", 1)
                    sbap = smallb[:, :]
                    d3h = bass.AP(sbap.tensor, sbap.offset + CB["dtm32"] + hh0,
                                  [list(sbap.ap[0]), [1, 16], [0, 64]])
                    v.tensor_tensor(tmp_tm[:, 0:1024], x_tm[:, hh0 * 64: hh0 * 64 + 1024],
                                    d3h, ALU.mult)
                    sm.inc(v.tensor_tensor(Y_tm[:, hh0 * 64: hh0 * 64 + 1024],
                                           Y_tm[:, hh0 * 64: hh0 * 64 + 1024],
                                           tmp_tm[:, 0:1024], ALU.add), "g7", 1)
                    for a in range(2):
                        ghg8 = 4 * c + 2 * half + a
                        v.wait_ge(sm.h["g13"], g0["g13"] + ghg8 + 1)
                        v.wait_ge(sm.h["g12"], g0["g12"] + ghg8 + 1)
                        ssl = S_sb[:, (hh0 + 8 * a) * 64: (hh0 + 8 * a) * 64 + 512]
                        v.tensor_tensor(ssl, ssl, p_dr, ALU.mult)
                        sm.inc(v.tensor_tensor(ssl, ssl, p_st, ALU.add), "g6", 1)
                for b in range(4):
                    n = 9 * c + 5 + b
                    v.wait_ge(sm.h["g8"], g0["g8"] + 4 * c + b + 1)
                    bank, cb0 = slotof(n)
                    xbap = xbc[:, :]
                    dst3 = bass.AP(xbap.tensor, xbap.offset + (4 * b) * T + c * 128,
                                   [list(xbap.ap[0]), [T, 4], [1, 128]])
                    sm.inc(v.tensor_copy(dst3, bank[:, cb0:cb0 + 512]), "g2", 1)

        @blk.scalar
        def _(s):
            s = EG(s)
            for c in range(NCH):
                for k in range(5):
                    n = 9 * c + k
                    s.wait_ge(sm.h["g1"], g0["g1"] + 6 * c + k + 1)
                    bank, cb0 = slotof(n)
                    if k < 4:
                        sm.inc(s.activation(x_tm[:, k * 512:(k + 1) * 512],
                                            bank[:, cb0:cb0 + 512], AF.Copy), "g2s", 1)
                    else:
                        sm.inc(s.activation(Br_tm[:, :], bank[:, cb0:cb0 + 128],
                                            AF.Copy), "g2s", 1)
                for g in range(8):
                    half, hg4 = divmod(g, 4)
                    gidx = 8 * c + g
                    s.wait_ge(sm.h["g3"], g0["g3"] + gidx + 1)
                    if gidx >= 2:
                        s.wait_ge(sm.h["g5"], g0["g5"] + gidx - 1)
                    for hi in range(4):
                        h = half * 16 + hg4 * 4 + hi
                        s.activation(expss[gidx % 2][:, hi * 128:(hi + 1) * 128],
                                     p_ss[:, hi * 128:(hi + 1) * 128], AF.Exp,
                                     bias=negacs[:, c * 32 + h:c * 32 + h + 1])
                    sm.inc(s.drain(), "g10", 1)

    pst = ExitStack()
    blk = nc.Block()
    blk.__enter__()
    if True:
        p_h = [pst.enter_context(nc.psum_tensor(f"p_h{i}", [128, 512], F32)) for i in range(8)]
        h1s0 = sm.snap("h1s"); h3s0 = sm.snap("h3s"); h4s0 = sm.snap("h4s"); h5s0 = sm.snap("h5s")
        wA0 = sm.snap("wA"); wB0 = sm.snap("wB")
        qmmH = sm.snap("qmm"); sqmH = sm.snap("sqm"); sqvH = sm.snap("sqv")
        rowsH = sm.snap("rows"); rowvH = sm.snap("rowv"); zmmH = sm.snap("zmm")
        hrow = xbc[0:1, 32768:34816]
        rstd3_row = xbc[0:1, 32768:34816]
        zscr = xbc[:, 34816:35328]
        sqscr = xbc[:, 35328:35840]
        sqscr2 = warena[:, 4096:4608]
        rowf3 = xbc[:, 35840:36864].bitcast(F32)
        r_mu3 = rowf3[0:1, 0:512]
        r_var3 = rowf3[32:33, 0:512]

        @blk.sync
        def _(sy):
            sm.inc(sy.dma_start(hrow, d_rows), "stm", 16)

        @blk.gpsimd
        def _(g):
            for gi in range(8):
                if gi >= 2:
                    g.wait_ge(sm.h["zmm"], zmmH + 8 * (gi - 1))
                par = gi % 2
                dst = warena[:, par * 2048:(par + 1) * 2048]
                srcap = bass.AP(i_wzzT.tensor, gi * 256,
                                [[DI, 128], [128 * DI, NK], [1, 256]])
                sm.inc(g.dma_start(dst.rearrange("p (k c) -> p k c", k=NK), srcap),
                       "wA" if par == 0 else "wB", 16)

        @blk.tensor
        def _(t):
            t.wait_ge(sm.h["stm"], sm.c["stm"])
            for tt in range(4):
                if tt >= 2:
                    t.wait_ge(sm.h["h1s"], h1s0 + tt - 1)
                sm.inc(t.matmul(p_h[6 + tt % 2][:, :], onesb[:, :],
                                hrow[:, tt * 512:(tt + 1) * 512], start=True, stop=True),
                       "qmm", 1)
            nz = 0
            for gi in range(8):
                par = gi % 2
                t.wait_ge(sm.h["wA" if par == 0 else "wB"],
                          (wA0 if par == 0 else wB0) + 16 * (gi // 2 + 1))
                t.wait_ge(sm.h["h1s"], h1s0 + 4)
                for jj in range(2):
                    for tt in range(4):
                        nz += 1
                        if nz > 4:
                            t.wait_ge(sm.h["h3s"], h3s0 + nz - 4)
                        bank = p_h[(nz - 1) % 4]
                        for k in range(NK):
                            mm = t.matmul(bank[:, :],
                                          warena[:, par * 2048 + k * 256 + jj * 128:
                                                 par * 2048 + k * 256 + jj * 128 + 128],
                                          hbuf[:, k * T + tt * 512: k * T + (tt + 1) * 512],
                                          start=(k == 0), stop=(k == NK - 1))
                        sm.inc(mm, "zmm", 1)
            idx = 0
            for tt in range(4):
                if tt >= 2:
                    t.wait_ge(sm.h["rowv"], rowvH + 2 * (tt - 2) + 1)
                for j in range(16):
                    t.wait_ge(sm.h["sqv"], sqvH + idx + 1)
                    mm = t.matmul(p_h[4 + tt % 2][0:1, :], onescol[:, :],
                                  sqscr if idx % 2 == 0 else sqscr2,
                                  start=(j == 0), stop=(j == 15))
                    sm.inc(mm, "sqm", 1)
                    idx += 1

        @blk.vector
        def _(v):
            v = EG(v)
            v.wait_ge(sm.h["stm"], sm.c["stm"])
            for tt in range(4):
                v.wait_ge(sm.h["qmm"], qmmH + tt + 1)
                for k in range(NK):
                    v.scalar_tensor_tensor(hbuf[:, k * T + tt * 512: k * T + (tt + 1) * 512],
                                           qb[:, k * T + tt * 512: k * T + (tt + 1) * 512],
                                           c_("npw", k), p_h[6 + tt % 2][:, :],
                                           ALU.mult, ALU.mult)
                sm.inc(v.drain(), "h1s", 1)
            nz = 0
            for gi in range(8):
                for jj in range(2):
                    j = gi * 2 + jj
                    for tt in range(4):
                        nz += 1
                        v.wait_ge(sm.h["h3s"], h3s0 + nz)
                        ysl = xbc[:, j * T + tt * 512: j * T + (tt + 1) * 512]
                        sm.inc(v.tensor_tensor(ysl, ysl, zscr, ALU.mult), "h4s", 1)
            idx = 0
            for tt in range(4):
                for j in range(16):
                    if idx >= 2:
                        v.wait_ge(sm.h["sqm"], sqmH + idx - 1)
                    ysl = xbc[:, j * T + tt * 512: j * T + (tt + 1) * 512]
                    sm.inc(v.tensor_mul(sqscr if idx % 2 == 0 else sqscr2, ysl, ysl),
                           "sqv", 1)
                    idx += 1
                v.wait_ge(sm.h["sqm"], sqmH + 16 * (tt + 1))
                v.tensor_scalar(r_mu3, p_h[4 + tt % 2][0:1, :], 1.0 / DI, EPS,
                                ALU.mult, ALU.add)
                sm.inc(v.drain(), "rowv", 1)
                v.wait_ge(sm.h["rows"], rowsH + tt + 1)
                sm.inc(v.tensor_copy(rstd3_row[:, tt * 512:(tt + 1) * 512], r_var3),
                       "rowv", 1)

        @blk.scalar
        def _(s):
            s = EG(s)
            for nz in range(1, 65):
                s.wait_ge(sm.h["zmm"], zmmH + nz)
                if nz >= 2:
                    s.wait_ge(sm.h["h4s"], h4s0 + nz - 1)
                bank = p_h[(nz - 1) % 4]
                sm.inc(s.activation(zscr, bank[:, :], AF.Silu), "h3s", 1)
            for tt in range(4):
                s.wait_ge(sm.h["rowv"], rowvH + 2 * tt + 1)
                s.activation(r_var3, r_mu3, AF.Ln)
                sm.inc(s.activation(r_var3, r_var3, AF.Exp, scale=-0.5), "rows", 1)

    if True:
        p_w = p_h[0:4]
        p_rI = p_h[4:8]
        wA1 = sm.snap("wA"); wB1 = sm.snap("wB"); wS0 = sm.snap("wS")
        i1s0 = sm.snap("h1s"); i2s0 = sm.snap("h2s"); i3s0 = sm.snap("h3s")
        zmmI = sm.snap("zmm"); qmmI = sm.snap("qmm"); rowvI = sm.snap("rowv")
        oscrA = [xbc[:, 34816 + i * 512: 34816 + (i + 1) * 512] for i in range(2)]
        oscrB = [xbc[:, 35840 + i * 512: 35840 + (i + 1) * 512] for i in range(2)]
        sfap = smallf[:, :]
        ngw3 = bass.AP(sfap.tensor, sfap.offset + COL["ngw"],
                       [list(sfap.ap[0]), [1, 16], [0, 256]])

        @blk.gpsimd
        def _(g):
            for gi in range(4):
                if gi < 2:
                    g.wait_ge(sm.h["zmm"], zmmI)
                else:
                    g.wait_ge(sm.h["zmm"], zmmI + 8 * (gi - 1))
                par = gi % 2
                dstb = warena[:, 0:4096] if par == 0 else hbuf[:, 0:4096]
                srcap = bass.AP(i_woutT.tensor, gi * 256,
                                [[D, 128], [128 * D, 16], [1, 256]])
                sm.inc(g.dma_start(dstb.rearrange("p (k c) -> p k c", k=16), srcap),
                       "wA" if par == 0 else "wB", 16)
            for k in range(NK):
                sm.inc(g.dma_start(wg_sb[:, k * 8:(k + 1) * 8],
                                   i_wgate[k * 128:(k + 1) * 128, :]), "cD", 16)

        @blk.sync
        def _(sy):
            sm.inc(sy.dma_start(rowtf[0:1, 0:8], i_bgate), "gtD", 16)
            sm.inc(sy.dma_start(rowtf[0:1, 8:9], i_alpha), "gtD", 16)
            sm.inc(sy.dma_start(c_("b1m", 0, 32), i_b1m), "gtD", 16)
            sm.inc(sy.dma_start(c_("b2m", 0, 8), i_b2m), "gtD", 16)

        @blk.tensor
        def _(t):
            for tt in range(4):
                t.wait_ge(sm.h["rowv"], rowvI - 8 + (5, 7, 6, 8)[tt])
                sm.inc(t.matmul(p_rI[tt][:, :], onesb[:, :],
                                rstd3_row[:, tt * 512:(tt + 1) * 512], start=True, stop=True),
                       "qmm", 1)
            nz = 0
            for gi in range(4):
                par = gi % 2
                t.wait_ge(sm.h["wS"], wS0 + gi + 1)
                wbuf = warena if par == 0 else hbuf
                for jj in range(2):
                    for tt in range(4):
                        nz += 1
                        if nz == 1:
                            t.wait_ge(sm.h["h3s"], i3s0)
                        if nz > 4:
                            t.wait_ge(sm.h["h1s"], i1s0 + nz - 4)
                        bank = p_w[(nz - 1) % 4]
                        for k in range(16):
                            mm = t.matmul(bank[:, :],
                                          wbuf[:, k * 256 + jj * 128: k * 256 + jj * 128 + 128],
                                          xbc[:, k * T + tt * 512: k * T + (tt + 1) * 512],
                                          start=(k == 0), stop=(k == 15))
                        sm.inc(mm, "zmm", 1)

        @blk.vector
        def _(v):
            v = EG(v)
            v.wait_ge(sm.h["h3s"], i3s0)
            def scale_w(gi):
                par = gi % 2
                v.wait_ge(sm.h["wA" if par == 0 else "wB"],
                          (wA1 if par == 0 else wB1) + 16 * (gi // 2 + 1))
                dstb = warena[:, 0:4096] if par == 0 else hbuf[:, 0:4096]
                sm.inc(v.tensor_tensor(dstb, dstb, ngw3, ALU.mult), "wS", 1)
            scale_w(0)
            scale_w(1)
            for nz in range(1, 33):
                j, tt = (nz - 1) // 4, (nz - 1) % 4
                v.wait_ge(sm.h["zmm"], zmmI + nz)
                if nz <= 4:
                    v.wait_ge(sm.h["qmm"], qmmI + tt + 1)
                    v.tensor_copy(S_sb[:, tt * 512:(tt + 1) * 512], p_rI[tt][:, :])
                if nz >= 3:
                    v.wait_ge(sm.h["h2s"], i2s0 + nz - 2)
                bank = p_w[(nz - 1) % 4]
                v.tensor_tensor(oscrA[(nz - 1) % 2], bank[:, :],
                                S_sb[:, tt * 512:(tt + 1) * 512], ALU.mult)
                sm.inc(v.tensor_tensor(oscrA[(nz - 1) % 2], oscrA[(nz - 1) % 2],
                                       qb[:, j * T + tt * 512: j * T + (tt + 1) * 512],
                                       ALU.add), "h1s", 1)
                if nz == 8:
                    scale_w(2)
                if nz == 16:
                    scale_w(3)
                if nz >= 2:
                    pz = nz - 1
                    pjj, ptt2 = (pz - 1) // 4, (pz - 1) % 4
                    v.wait_ge(sm.h["h2s"], i2s0 + pz)
                    xsl = x0b[:, pjj * T + ptt2 * 512: pjj * T + (ptt2 + 1) * 512]
                    sm.inc(v.tensor_tensor(xsl, xsl, oscrB[(pz - 1) % 2], ALU.add), "h3s", 1)
            v.wait_ge(sm.h["h2s"], i2s0 + 32)
            xsl = x0b[:, 7 * T + 3 * 512: 7 * T + 4 * 512]
            sm.inc(v.tensor_tensor(xsl, xsl, oscrB[(32 - 1) % 2], ALU.add), "h3s", 1)

        @blk.scalar
        def _(s):
            s = EG(s)
            for nz in range(1, 33):
                j = (nz - 1) // 4
                s.wait_ge(sm.h["h1s"], i1s0 + nz)
                if nz >= 3:
                    s.wait_ge(sm.h["h3s"], i3s0 + nz - 2)
                sm.inc(s.activation(oscrB[(nz - 1) % 2], oscrA[(nz - 1) % 2],
                                    AF.Copy, scale=c_("ga", j)), "h2s", 1)

    if True:
        jk_banks = p_h
        p_s2a = [jk_banks[4][0:1, :], jk_banks[5][0:1, :]]
        p_q2a = [jk_banks[6][0:1, :], jk_banks[7][0:1, :]]
        p_rJ = jk_banks[0:4]
        sqvJ = sm.snap("sqv"); sqmJ = sm.snap("sqm"); rowvJ = sm.snap("rowv")
        rowsJ = sm.snap("rows"); qmmJ = sm.snap("qmm"); h1sJ = sm.snap("h1s")
        h2sJ = sm.snap("h2s"); gtDJ = sm.snap("gtD"); cDJ = sm.snap("cD")
        snap["Jg"] = (qmmJ, gtDJ, rowsJ)
        mu2_row = xbc[0:1, 32768:34816]
        rs2_row = xbc[0:1, 30720:32768]
        sqscrJ = xbc[:, 35328:35840]
        sqscrJ2 = warena[:, 4096:4608]
        rowfJ = xbc[:, 35840:36864].bitcast(F32)
        r_muJ = rowfJ[0:1, 0:512]
        r_varJ = rowfJ[32:33, 0:512]
        musqJ = xbc[:, 30720:31744].bitcast(F32)[32:33, 0:512]
        applyscr = xbc[:, 34816:35328]

        @blk.sync
        def _(sy):
            sy.wait_ge(sm.h["h2s"], h2sJ + 2)
            with sy.register("re1") as re1, sy.register("re2") as re2, \
                 sy.register("ro") as ro:
                sy.reg_load(re1, idx8[0:1, 0:1])
                sy.reg_load(re2, idx8[0:1, 1:2])
                sy.reg_mul(ro, re1, 4096)
                sm.inc(sy.dma_start(c_("b1e1", 0, 32),
                                    bass.AP(i_b1e.tensor, ro, [[32, 128], [1, 32]])),
                       "gt2", 16)
                sy.reg_mul(ro, re2, 4096)
                sm.inc(sy.dma_start(c_("b1e2", 0, 32),
                                    bass.AP(i_b1e.tensor, ro, [[32, 128], [1, 32]])),
                       "gt2", 16)
                sy.reg_mul(ro, re1, 1024)
                sm.inc(sy.dma_start(c_("b2e1", 0, 8),
                                    bass.AP(i_b2e.tensor, ro, [[8, 128], [1, 8]])),
                       "gt2", 16)
                sy.reg_mul(ro, re2, 1024)
                sm.inc(sy.dma_start(c_("b2e2", 0, 8),
                                    bass.AP(i_b2e.tensor, ro, [[8, 128], [1, 8]])),
                       "gt2", 16)

        @blk.vector
        def _(v):
            v = EG(v)
            idx = 0
            for tt in range(4):
                for k in range(NK):
                    if idx >= 2:
                        v.wait_ge(sm.h["sqm"], sqmJ + idx - 1)
                    xs = x0b[:, k * T + tt * 512: k * T + (tt + 1) * 512]
                    sm.inc(v.tensor_mul(sqscrJ if idx % 2 == 0 else sqscrJ2, xs, xs),
                           "sqv", 1)
                    idx += 1
                v.wait_ge(sm.h["sqm"], sqmJ + 8 * (tt + 1))
                v.tensor_scalar(r_muJ, p_q2a[tt % 2][:, :], 1.0 / D, None, ALU.mult)
                v.tensor_copy(mu2_row[:, tt * 512:(tt + 1) * 512], r_muJ)
                v.tensor_scalar(r_varJ, p_s2a[tt % 2][:, :], 1.0 / D, None, ALU.mult)
                v.drain()
                v.tensor_mul(rowfJ[0:1, 512:1024] if False else r_varJ, r_varJ, r_varJ) if False else None
                v.tensor_mul(musqJ, r_muJ, r_muJ)
                v.tensor_sub(r_varJ, r_varJ, musqJ)
                v.tensor_scalar(r_varJ, r_varJ, 1.0, EPS, ALU.mult, ALU.add)
                sm.inc(v.drain(), "rowv", 1)
                v.wait_ge(sm.h["rows"], rowsJ + tt + 1)
                sm.inc(v.tensor_copy(rs2_row[:, tt * 512:(tt + 1) * 512], r_varJ), "rowv", 1)
            for tt in range(4):
                v.wait_ge(sm.h["qmm"], qmmJ + tt + 1)
                for k in range(NK):
                    xs = x0b[:, k * T + tt * 512: k * T + (tt + 1) * 512]
                    v.tensor_sub(applyscr, xs, p_rJ[2 * (tt % 2)][:, :])
                    v.tensor_mul(applyscr, applyscr, p_rJ[2 * (tt % 2) + 1][:, :])
                    v.tensor_scalar(qb[:, k * T + tt * 512: k * T + (tt + 1) * 512],
                                    applyscr, c_("onepcb", k), c_("mod", 24 + k),
                                    ALU.mult, ALU.add)
                sm.inc(v.drain(), "h1s", 1)
            for k in range(NK):
                v.tensor_reduce(c_("meancol", k), qb[:, k * T:(k + 1) * T], AX, ALU.add)
            v.drain()
            v.tensor_scalar(c_("meancol", 0, NK), c_("meancol", 0, NK), 1.0 / T, None, ALU.mult)
            v.drain()
            v.wait_ge(sm.h["cD"], cDJ)
            sm.inc(v.tensor_copy(cb_("mean8", 0, NK), c_("meancol", 0, NK)), "h1s", 1)

        @blk.scalar
        def _(s):
            s = EG(s)
            for tt in range(4):
                s.wait_ge(sm.h["rowv"], rowvJ + 2 * tt + 1)
                s.activation(r_varJ, r_varJ, AF.Ln)
                sm.inc(s.activation(r_varJ, r_varJ, AF.Exp, scale=-0.5), "rows", 1)

        @blk.tensor
        def _(t):
            idx = 0
            for tt in range(4):
                if tt >= 2:
                    t.wait_ge(sm.h["rowv"], rowvJ + 2 * (tt - 2) + 1)
                for k in range(NK):
                    t.wait_ge(sm.h["sqv"], sqvJ + idx + 1)
                    t.matmul(p_q2a[tt % 2][:, :], onescol[:, :],
                             x0b[:, k * T + tt * 512: k * T + (tt + 1) * 512],
                             start=(k == 0), stop=(k == NK - 1))
                    mm = t.matmul(p_s2a[tt % 2][:, :], onescol[:, :],
                                  sqscrJ if idx % 2 == 0 else sqscrJ2,
                                  start=(k == 0), stop=(k == NK - 1))
                    sm.inc(mm, "sqm", 1)
                    idx += 1
            for tt in range(4):
                t.wait_ge(sm.h["rowv"], rowvJ + 2 * tt + 2)
                if tt >= 2:
                    t.wait_ge(sm.h["h1s"], h1sJ + tt - 1)
                t.matmul(p_rJ[2 * (tt % 2)][:, :], onesb[:, :],
                         mu2_row[:, tt * 512:(tt + 1) * 512], start=True, stop=True)
                sm.inc(t.matmul(p_rJ[2 * (tt % 2) + 1][:, :], onesb[:, :],
                                rs2_row[:, tt * 512:(tt + 1) * 512], start=True, stop=True),
                       "qmm", 1)

    if True:
        p_f = jk_banks
        eA0 = sm.snap("eAD"); eB0 = sm.snap("eBD")
        wA2 = sm.snap("wA"); wB2 = sm.snap("wB")
        f1s0 = sm.snap("h4s")
        f2s0 = sm.snap("h5s")
        f3s0 = sm.snap("h3s")
        f4s0 = sm.snap("h1s")
        zmmK = sm.snap("zmm")
        qmmK = sm.snap("qmm")
        oD0 = sm.snap("oD")
        oE0 = sm.snap("oE")
        h2sJ3 = sm.snap("h2s")
        A2sb = hbuf[:, :].bitcast(F32)
        negbc = negacs[:, :].bitcast(BF16)
        t3slot = negbc[:, 0:512]
        x2slot = [negbc[:, 512:1024], warena[:, 4096:4608]]
        w2B = [S_sb[:, i * 128:(i + 1) * 128] for i in range(16)] + \
              [tmdt[:, i * 128:(i + 1) * 128] for i in range(8)] + \
              [expacs.bitcast(BF16)[:, i * 128:(i + 1) * 128] for i in range(8)]

        def w1buf(par):
            return xbc[:, 32768 + par * 2048: 32768 + (par + 1) * 2048]

        NA1 = 16 * 2 * 2
        NA2 = 8 * 2
        p_lg = p_f[7][0:1, 0:8]
        p_cf = p_f[7][:, 8:12]
        qmmJg, gtDJg, rowsJg = snap["Jg"]

        gwsem = {(0, 1): "gwD", (0, 2): "gw2", (1, 1): "gw3", (1, 2): "gw4"}
        gw0 = {k: sm.snap(v) for k, v in gwsem.items()}
        W1N = 16
        W2N = 64

        @blk.gpsimd
        def _(g):
            with g.register("je") as je, g.register("jo") as jo:
                gathered = False

                def emit_gathers():
                    g.wait_ge(sm.h["h2s"], h2sJ3 + 2)
                    for s in range(2):
                        g.reg_load(je, idx8[0:1, s:s + 1])
                        g.reg_mul(je, je, D * 4096)
                        sm.inc(g.dma_start(
                            bass.AP(d_w1b.tensor, s * D * 4096,
                                    [[4096, 1024], [1, 4096]]),
                            bass.AP(i_w1e.tensor, je, [[4096, 1024], [1, 4096]])),
                               gwsem[(s, 1)], 16)
                        for r in range(4):
                            g.reg_add(jo, je, r * 1024 * D)
                            sm.inc(g.dma_start(
                                bass.AP(d_w2b.tensor, s * 4096 * D + r * 1024 * D,
                                        [[D, 1024], [1, 1024]]),
                                bass.AP(i_w2e.tensor, jo, [[D, 1024], [1, 1024]])),
                                   gwsem[(s, 2)], 16)

                for hf in range(2):
                    if hf == 1:
                        for j2 in range(8):
                            for t2 in range(2):
                                oidx = (0 * 8 + j2) * 2 + t2
                                g.wait_ge(sm.h["h3s"], f3s0 + oidx + 1)
                                sm.inc(g.dma_start(
                                    o_out[j2 * 128:(j2 + 1) * 128,
                                          0 * 1024 + t2 * 512: (t2 + 1) * 512],
                                    x2slot[oidx % 2]), "oD" if oidx % 2 == 0 else "oE", 16)
                    for p in range(3):
                        for grp in range(16):
                            par = grp % 2
                            gidx = (hf * 3 + p) * 16 + grp
                            if gidx == 0:
                                g.wait_ge(sm.h["qmm"], qmmK)
                            if gidx == 1:
                                g.wait_ge(sm.h["h1s"], f4s0 - 1)
                            if gidx == 10 and not gathered:
                                emit_gathers()
                                gathered = True
                            if gidx >= 2:
                                g.wait_ge(sm.h["zmm"], zmmK + 4 * (gidx - 1))
                            dst = w1buf(par).rearrange("p (k c) -> p k c", k=NK)
                            if p == 0:
                                srcap = bass.AP(i_w1m.tensor, grp * 256,
                                                [[4096, 128], [128 * 4096, NK], [1, 256]])
                            else:
                                g.wait_ge(sm.h[gwsem[(p - 1, 1)]],
                                          gw0[(p - 1, 1)] + W1N)
                                srcap = bass.AP(d_w1b.tensor,
                                                (p - 1) * D * 4096 + grp * 256,
                                                [[4096, 128], [128 * 4096, NK], [1, 256]])
                            sm.inc(g.dma_start(dst, srcap),
                                   "eAD" if par == 0 else "eBD", 16)
                        for j2 in range(8):
                            par = j2 % 2
                            sidx = (hf * 3 + p) * 8 + j2
                            if sidx == 0:
                                g.wait_ge(sm.h["zmm"], zmmK)
                            if sidx == 1:
                                g.wait_ge(sm.h["h1s"], f4s0 - 5)
                            if sidx >= 2:
                                g.wait_ge(sm.h["qmm"], qmmK + 4 + 2 * (sidx - 1))
                            if p >= 1:
                                g.wait_ge(sm.h[gwsem[(p - 1, 2)]],
                                          gw0[(p - 1, 2)] + W2N)
                            if par == 0:
                                dst = warena[:, 0:4096].rearrange("p (k c) -> p k c", k=32)
                                if p == 0:
                                    srcap = bass.AP(i_w2m.tensor, j2 * 128,
                                                    [[D, 128], [128 * D, 32], [1, 128]])
                                else:
                                    srcap = bass.AP(d_w2b.tensor,
                                                    (p - 1) * 4096 * D + j2 * 128,
                                                    [[D, 128], [128 * D, 32], [1, 128]])
                                sm.inc(g.dma_start(dst, srcap), "wA", 16)
                            else:
                                for piece, (ks, ke) in enumerate(((0, 16), (16, 24), (24, 32))):
                                    if p == 0:
                                        srcap = bass.AP(i_w2m.tensor, j2 * 128 + ks * 128 * D,
                                                        [[D, 128], [128 * D, ke - ks], [1, 128]])
                                    else:
                                        srcap = bass.AP(d_w2b.tensor,
                                                        (p - 1) * 4096 * D + j2 * 128 + ks * 128 * D,
                                                        [[D, 128], [128 * D, ke - ks], [1, 128]])
                                    if piece == 0:
                                        dstap = S_sb[:, 0:2048].rearrange("p (k c) -> p k c", k=16)
                                    elif piece == 1:
                                        dstap = tmdt[:, 0:1024].rearrange("p (k c) -> p k c", k=8)
                                    else:
                                        dstap = expacs[:, :].bitcast(BF16).rearrange("p (k c) -> p k c", k=8)
                                    sm.inc(g.dma_start(dstap, srcap), "wB", 16)
                for j2 in range(8):
                    for t2 in range(2):
                        oidx = (1 * 8 + j2) * 2 + t2
                        g.wait_ge(sm.h["h3s"], f3s0 + oidx + 1)
                        sm.inc(g.dma_start(
                            o_out[j2 * 128:(j2 + 1) * 128,
                                  1024 + t2 * 512: 1024 + (t2 + 1) * 512],
                            x2slot[oidx % 2]), "oD" if oidx % 2 == 0 else "oE", 16)

        @blk.tensor
        def _(t):
            na1 = 0
            na2 = 0
            for hf in range(2):
                for p in range(3):
                    pidx = hf * 3 + p
                    for grp in range(16):
                        par = grp % 2
                        gidx = pidx * 16 + grp
                        if gidx == 5:
                            t.wait_ge(sm.h["h1s"], f4s0)
                            t.wait_ge(sm.h["cD"], sm.c["cD"])
                            for k in range(NK):
                                mm = t.matmul(p_lg, cb_("mean8", k, 1),
                                              wg_sb[:, k * 8:(k + 1) * 8],
                                              start=(k == 0), stop=(k == NK - 1))
                            sm.inc(mm, "qmm", 1)
                        if gidx == 10:
                            t.wait_ge(sm.h["h2s"], h2sJ3 + 2)
                            for i in range(3):
                                sm.inc(t.matmul(p_cf[:, i:i + 1], onesb[:, :],
                                                rowtb[0:1, i:i + 1],
                                                start=True, stop=True), "qmm", 1)
                        t.wait_ge(sm.h["eAD" if par == 0 else "eBD"],
                                  (eA0 if par == 0 else eB0) + 16 * (gidx // 2 + 1))
                        for jj in range(2):
                            j = grp * 2 + jj
                            for t2 in range(2):
                                na1 += 1
                                if na1 == 1:
                                    t.wait_ge(sm.h["h1s"], f4s0 - 1)
                                if na1 > 4:
                                    t.wait_ge(sm.h["h4s"], f1s0 + na1 - 4)
                                bank = p_f[(na1 - 1) % 4]
                                for k in range(NK):
                                    mm = t.matmul(bank[:, :],
                                                  w1buf(par)[:, k * 256 + jj * 128: k * 256 + jj * 128 + 128],
                                                  qb[:, k * T + hf * 1024 + t2 * 512:
                                                     k * T + hf * 1024 + (t2 + 1) * 512],
                                                  start=(k == 0), stop=(k == NK - 1))
                                sm.inc(mm, "zmm", 1)
                    for j2 in range(8):
                        par = j2 % 2
                        sidx = pidx * 8 + j2
                        t.wait_ge(sm.h["wA" if par == 0 else "wB"],
                                  (wA2 if par == 0 else wB2) +
                                  (16 * (sidx // 2 + 1) if par == 0 else 48 * (sidx // 2 + 1)))
                        for t2 in range(2):
                            na2 += 1
                            if na2 == 4:
                                t.wait_ge(sm.h["h2s"], h2sJ3 + 3)
                            if na2 > 4:
                                t.wait_ge(sm.h["h5s"], f2s0 + na2 - 4)
                            t.wait_ge(sm.h["h4s"], f1s0 + pidx * 64 + 64)
                            if p == 0:
                                t.wait_ge(sm.h["h1s"], f4s0 + hf * 8 + j2 + 1)
                            bank = p_f[4 + (na2 - 1) % 4]
                            for k in range(32):
                                wtile = (warena[:, k * 128:(k + 1) * 128] if par == 0
                                         else w2B[k])
                                mm = t.matmul(bank[:, :], wtile,
                                              xbc[:, k * 1024 + t2 * 512: k * 1024 + (t2 + 1) * 512],
                                              start=(k == 0), stop=(k == 31))
                            sm.inc(mm, "qmm", 1)

        @blk.scalar
        def _(s):
            s = EG(s)
            s.wait_ge(sm.h["gtD"], sm.c["gtD"])
            na1 = 0
            expert_bias_ok = False
            for hf in range(2):
                for p in range(3):
                    if p >= 1 and not expert_bias_ok:
                        s.wait_ge(sm.h["gt2"], sm.c["gt2"])
                        expert_bias_ok = True
                    bcol = {0: "b1m", 1: "b1e1", 2: "b1e2"}[p]
                    for grp in range(16):
                        for jj in range(2):
                            j = grp * 2 + jj
                            for t2 in range(2):
                                na1 += 1
                                if na1 == 20:
                                    s.wait_ge(sm.h["h2s"], h2sJ3 + 1)
                                    sm.inc(s.activation(rowtf[0:1, 25:26],
                                                        rowtf[0:1, 24:25], AF.Exp,
                                                        scale=-1.0), "rows", 1)
                                s.wait_ge(sm.h["zmm"], zmmK + na1)
                                bank = p_f[(na1 - 1) % 4]
                                sm.inc(s.activation(
                                    xbc[:, j * 1024 + t2 * 512: j * 1024 + (t2 + 1) * 512],
                                    bank[:, :], AF.Gelu_apprx_tanh,
                                    bias=c_(bcol, j)), "h4s", 1)
                for j2 in range(8):
                    for t2 in range(2):
                        oidx = (hf * 8 + j2) * 2 + t2
                        s.wait_ge(sm.h["h5s"], f2s0 + (hf * 3 + 2) * 16 + (j2 * 2 + t2) + 1)
                        if oidx >= 1:
                            s.wait_ge(sm.h["h3s"], f3s0 + oidx)
                        sm.inc(s.activation(t3slot,
                                            A2sb[:, j2 * 1024 + t2 * 512: j2 * 1024 + (t2 + 1) * 512],
                                            AF.Copy, scale=c_("gb", j2)), "h2s", 1)

        @blk.vector
        def _(v):
            v = EG(v)
            v.wait_ge(sm.h["qmm"], qmmJg + 5)
            v.wait_ge(sm.h["gtD"], gtDJg)
            lgrow = rowtf[0:1, 16:24]
            v.tensor_tensor(lgrow, p_lg, rowtf[0:1, 0:8], ALU.add)
            v.drain()
            v.max(top8[:, :], lgrow)
            v.drain()
            v.max_index(idx8[:, :], top8[:, :], lgrow)
            v.drain()
            d12 = rowtf[0:1, 24:25]
            v.tensor_sub(d12, top8[0:1, 0:1], top8[0:1, 1:2])
            sm.inc(v.drain(), "h2s", 1)
            v.wait_ge(sm.h["rows"], rowsJg + 5)
            g1 = rowtf[0:1, 25:26]
            v.tensor_scalar(g1, g1, 1.0, None, ALU.add)
            v.reciprocal(g1, g1)
            g2 = rowtf[0:1, 26:27]
            ac = rowtf[0:1, 27:28]
            amc = rowtf[0:1, 28:29]
            ce1 = rowtf[0:1, 29:30]
            ce2 = rowtf[0:1, 30:31]
            v.tensor_scalar(g2, g1, -1.0, 1.0, ALU.mult, ALU.add)
            v.tensor_scalar(ac, rowtf[0:1, 8:9], 0.1, 1.0, ALU.max, ALU.min)
            v.tensor_scalar(amc, ac, -1.0, 1.0, ALU.mult, ALU.add)
            v.tensor_mul(ce1, amc, g1)
            v.tensor_mul(ce2, amc, g2)
            v.tensor_copy(rowtb[0:1, 0:1], ac)
            v.tensor_copy(rowtb[0:1, 1:2], ce1)
            v.tensor_copy(rowtb[0:1, 2:3], ce2)
            sm.inc(v.drain(), "h2s", 1)
            v.wait_ge(sm.h["qmm"], qmmJg + 8)
            v.tensor_copy(c_("coefm"), p_cf[:, 0:1])
            v.tensor_copy(c_("coefe1"), p_cf[:, 1:2])
            v.tensor_copy(c_("coefe2"), p_cf[:, 2:3])
            v.drain()
            v.wait_ge(sm.h["gtD"], sm.c["gtD"])
            v.wait_ge(sm.h["gt2"], sm.c["gt2"])
            v.tensor_scalar(c_("bcomb", 0, 8), c_("b2m", 0, 8), c_("coefm"), None, ALU.mult)
            v.drain()
            v.scalar_tensor_tensor(c_("bcomb", 0, 8), c_("b2e1", 0, 8),
                                   c_("coefe1"), c_("bcomb", 0, 8), ALU.mult, ALU.add)
            v.drain()
            sm.inc(v.scalar_tensor_tensor(c_("bcomb", 0, 8), c_("b2e2", 0, 8),
                                          c_("coefe2"), c_("bcomb", 0, 8),
                                          ALU.mult, ALU.add), "h2s", 1)
            na2 = 0
            for hf in range(2):
                v.wait_ge(sm.h["h2s"], h2sJ3 + 3)
                for j2 in range(8):
                    if hf == 1:
                        v.wait_ge(sm.h["h2s"], h2sJ3 + 3 + 16 * hf)
                    sm.inc(v.tensor_copy(A2sb[:, j2 * 1024:(j2 + 1) * 1024],
                                         c_("bcomb", j2).broadcast_to([128, 1024])),
                           "h1s", 1)
                for p in range(3):
                    pidx = hf * 3 + p
                    coefn = {0: "coefm", 1: "coefe1", 2: "coefe2"}[p]
                    for j2 in range(8):
                        for t2 in range(2):
                            na2 += 1
                            v.wait_ge(sm.h["qmm"], qmmK + 4 + na2)
                            bank = p_f[4 + (na2 - 1) % 4]
                            asl = A2sb[:, j2 * 1024 + t2 * 512: j2 * 1024 + (t2 + 1) * 512]
                            sm.inc(v.scalar_tensor_tensor(asl, bank[:, :], c_(coefn),
                                                          asl, ALU.mult, ALU.add), "h5s", 1)
                for j2 in range(8):
                    for t2 in range(2):
                        oidx = (hf * 8 + j2) * 2 + t2
                        v.wait_ge(sm.h["h2s"], h2sJ3 + 3 + oidx + 1)
                        if oidx >= 2:
                            v.wait_ge(sm.h["oD" if oidx % 2 == 0 else "oE"],
                                      (oD0 if oidx % 2 == 0 else oE0) + 16 * ((oidx - 2) // 2 + 1))
                        sm.inc(v.tensor_tensor(x2slot[oidx % 2],
                                               x0b[:, j2 * T + hf * 1024 + t2 * 512:
                                                   j2 * T + hf * 1024 + (t2 + 1) * 512],
                                               t3slot, ALU.add), "h3s", 1)

    pst.close()
    blk.__exit__(None, None, None)

    with nc.Block() as blk:
        @blk.sync
        def _(sy):
            sy.wait_ge(sm.h["oD"], sm.c["oD"])
            sy.wait_ge(sm.h["oE"], sm.c["oE"])
            sy.wait_ge(sm.h["st"], sm.c["st"])

    with nc.Block() as blk:
        @blk.sync
        def _(sy):
            if db_mod is not None:
                sm.inc(sy.dma_start(db_mod, smallf[:, COL["mod"]:COL["mod"] + 48]), "st", 16)
            if db_q is not None:
                sm.inc(sy.dma_start(db_q, qb[:, :]), "st", 16)
            if db_hin is not None:
                sm.inc(sy.dma_start(db_hin, hbuf[:, :]), "st", 16)
            if db_cos is not None:
                sm.inc(sy.dma_start(db_cos, cosrep[:, :]), "st", 16)
            if db_sin is not None:
                sm.inc(sy.dma_start(db_sin, sinrep2[:, :]), "st", 16)
            if db_xbcc is not None:
                sm.inc(sy.dma_start(db_xbcc, xbc[:, :]), "st", 16)
            if db_dt is not None:
                sm.inc(sy.dma_start(db_dt, dt_fm), "st", 16)
            if db_acs is not None:
                sm.inc(sy.dma_start(db_acs, acs_fm), "st", 16)
            if db_tm is not None:
                sm.inc(sy.dma_start(db_tm, tmdt[:, :]), "st", 16)
            if db_y is not None:
                sm.inc(sy.dma_start(db_y, xbc[:, 0:16 * T]), "st", 16)
            if db_x1 is not None:
                sm.inc(sy.dma_start(db_x1, x0b[:, :]), "st", 16)
            sy.wait_ge(sm.h["st"], sm.snap("st"))

    ctx.close()
    return nc, dbg_outs


def stage_core_inputs(inputs, b):
    A = lambda x: np.ascontiguousarray(x, dtype=np.float32)
    d = {}
    d["xT"] = A(inputs["L0_attn_x"][b].T)
    d["wqT"] = A(inputs["L0_attn_W_in"][:D].T)
    d["npw"] = A(inputs["L0_attn_norm_pre_w"].reshape(NK, 128).T)
    d["ccol"] = A(inputs["L0_c"][b].reshape(NK, 128).T)
    d["wmod"] = A(inputs["L0_wmod"])
    d["identf"] = np.eye(128, dtype=np.float32)
    d["wzxcT"] = A(inputs["L0_attn_W_zxbcdt"][DI:DI + DTW].T)
    d["convw"] = A(inputs["L0_attn_conv_w"].reshape(NJC, 128, 4).transpose(1, 0, 2).reshape(128, NJC * 4))
    d["convb"] = A(inputs["L0_attn_conv_b"].reshape(NJC, 128).T)
    d["dtb"] = A(inputs["L0_attn_dt_bias"].reshape(H, 1))
    d["alog"] = A(inputs["L0_attn_A_log"].reshape(H, 1))
    d["trilneg"] = np.tril(np.full((128, 128), -30000.0, dtype=np.float32), -1)
    d["drep"] = A(np.repeat(inputs["L0_attn_D_param"], 64).reshape(16, 128).T)
    d["dtm32"] = A(np.broadcast_to(np.asarray(inputs["L0_attn_D_param"]).reshape(1, 32), (128, 32)))
    d["wzzT"] = A(inputs["L0_attn_W_zxbcdt"][0:DI].T)
    d["woutT"] = A(inputs["L0_attn_W_out"].T)
    d["ngw"] = A(inputs["L0_attn_norm_gated_w"].reshape(16, 128).T)
    d["wgate"] = A(inputs["L0_ffn_w_gate"])
    d["bgate"] = A(inputs["L0_ffn_b_gate"].reshape(1, 8))
    d["alpha"] = A(np.asarray(inputs["L0_ffn_alpha"]).reshape(1, 1))
    d["w1m"] = A(inputs["L0_ffn_W1m"])
    d["b1m"] = A(inputs["L0_ffn_b1m"].reshape(32, 128).T)
    d["w2m"] = A(inputs["L0_ffn_W2m"])
    d["b2m"] = A(inputs["L0_ffn_b2m"].reshape(8, 128).T)
    d["w1e"] = A(inputs["L0_ffn_W1e"].reshape(8 * D, 4096))
    d["b1e"] = A(inputs["L0_ffn_b1e"].reshape(8, 32, 128).transpose(0, 2, 1).reshape(8 * 128, 32))
    d["w2e"] = A(inputs["L0_ffn_W2e"].reshape(8 * 4096, D))
    d["b2e"] = A(inputs["L0_ffn_b2e"].reshape(8, 8, 128).transpose(0, 2, 1).reshape(8 * 128, 8))
    d["iota32"] = np.arange(H, dtype=np.float32).reshape(H, 1)
    return d



def kernel(**inputs):
    inputs = {k: np.asarray(v) for k, v in inputs.items()}
    nc, _ = build_core_program(())
    in_maps = [stage_core_inputs(inputs, b) for b in range(8)]
    res = run_bass_kernel_spmd(nc, in_maps, core_ids=list(range(8)))
    out = np.stack([np.ascontiguousarray(res.results[b]["o_xT"].T) for b in range(8)], axis=0)
    return out.astype(np.float32)


# version 18
# speedup vs baseline: 1.0024x; 1.0024x over previous
import math
import sys
import numpy as np
from contextlib import ExitStack

sys.path.insert(0, "/opt/trn_rl_repo")
import concourse.bass as bass
import concourse.mybir as mybir
from concourse.bass_utils import run_bass_kernel_spmd

F32 = mybir.dt.float32
BF16 = mybir.dt.bfloat16
U32 = mybir.dt.uint32
AF = mybir.ActivationFunctionType
ALU = mybir.AluOpType
AX = mybir.AxisListType.X

T, D, NK, DI = 2048, 1024, 8, 2048
CONVD, NJC, DTW = 2304, 18, 2336
H, CH, NCH = 32, 128, 16
EPS = 1e-5

SEM_NAMES = ["cD", "x0D", "wmA", "wmB", "wmC", "wmD", "mcst", "xD", "qD", "mmod", "mev", "sqv", "sqm",
             "stm", "rowv", "rows", "qmm", "qev", "hv", "st", "misc",
             "wzA", "wzB", "zmm", "zev", "dtv", "dts", "dtm", "cvm", "cvv", "cvs", "x2D",
             "g1", "g2", "g3", "g4", "g5", "g6", "g7", "g8", "g9", "g10", "g12", "g13",
             "h1s", "h2s", "h3s", "h4s", "h5s", "wA", "wB", "gtD", "gt2", "gwD",
             "gw2", "gw3", "gw4", "wS", "g2s", "eAD", "eBD", "oD", "oE"]



class EG:
    OPNAMES = {"tensor_tensor", "tensor_add", "tensor_sub", "tensor_mul", "tensor_max",
               "tensor_copy", "tensor_scalar", "tensor_single_scalar", "tensor_scalar_add",
               "tensor_scalar_sub", "tensor_scalar_mul", "tensor_scalar_max", "tensor_scalar_min",
               "scalar_tensor_tensor", "tensor_tensor_scan", "tensor_reduce", "reduce_sum",
               "reduce_max", "reciprocal", "memset", "memzero", "select", "max", "max_index",
               "iota", "activation", "copy", "add", "mul", "square", "sqrt", "sign",
               "transpose", "stream_shuffle"}

    def __init__(self, eng):
        self.e = eng
        self.w = []
        self.r = []

    @staticmethod
    def _span(ap):
        if not isinstance(ap, bass.AP):
            return None
        off = ap.offset
        if not isinstance(off, int):
            return (ap.tensor.name, -10**15, 10**15)
        ext = 1
        for st, ct in ap.ap:
            ext += abs(int(st)) * (int(ct) - 1)
        es = mybir.dt.size(ap.dtype)
        return (ap.tensor.name, off * es, (off + ext) * es)

    @staticmethod
    def _overlap(a, b):
        return a[0] == b[0] and a[1] < b[2] and b[1] < a[2]

    def drain(self):
        ins = self.e.drain()
        self.w = []
        self.r = []
        return ins

    def wait_ge(self, sem, val):
        return self.e.wait_ge(sem, val)

    def dma_start(self, *a, **kw):
        return self.e.dma_start(*a, **kw)

    def __getattr__(self, name):
        fn = getattr(self.e, name)
        if name not in EG.OPNAMES:
            return fn

        def wrapped(*args, **kwargs):
            aps = [a for a in args if isinstance(a, bass.AP)]
            if "accum_out" in kwargs and isinstance(kwargs["accum_out"], bass.AP):
                aps.append(kwargs["accum_out"])
            for k in ("bias", "scale"):
                if isinstance(kwargs.get(k), bass.AP):
                    aps.append(kwargs[k])
            outs = [aps[0]] if aps else []
            if "accum_out" in kwargs and isinstance(kwargs["accum_out"], bass.AP):
                outs.append(kwargs["accum_out"])
            ins = aps[1:]
            ospans = [s for s in (EG._span(a) for a in outs) if s]
            ispans = [s for s in (EG._span(a) for a in ins) if s]
            need = False
            for s in ispans + ospans:
                for wsp in self.w:
                    if EG._overlap(s, wsp):
                        need = True
            for s in ospans:
                for rsp in self.r:
                    if EG._overlap(s, rsp):
                        need = True
            if need:
                self.e.drain()
                self.w = []
                self.r = []
            instr = fn(*args, **kwargs)
            self.w.extend(ospans)
            self.r.extend(ispans)
            return instr

        return wrapped


class Sems:
    def __init__(self, nc, ctx, names):
        self.h = {n: ctx.enter_context(nc.semaphore(n)) for n in names}
        self.c = {n: 0 for n in names}

    def inc(self, instr, n, k=1):
        instr.then_inc(self.h[n], k)
        self.c[n] += k
        return instr

    def snap(self, n):
        return self.c[n]


def build_core_program(debug=()):
    nc = bass.Bass("TRN2", target_bir_lowering=False)
    dbg = set(debug)

    dram = lambda n, s, dt=F32: nc.dram_tensor(n, s, dt, kind="ExternalInput").ap()
    i_xT = dram("xT", [D, T])
    i_wqT = dram("wqT", [D, D])
    i_npw = dram("npw", [128, NK])
    i_ccol = dram("ccol", [128, NK])
    i_wmod = dram("wmod", [D, 6 * D])
    i_identf = dram("identf", [128, 128])
    i_iota32 = dram("iota32", [H, 1])
    i_wzxcT = dram("wzxcT", [D, DTW])
    i_convw = dram("convw", [128, NJC * 4])
    i_convb = dram("convb", [128, NJC])
    i_dtb = dram("dtb", [H, 1])
    i_alog = dram("alog", [H, 1])
    i_drep = dram("drep", [128, 16])
    i_dtm32 = dram("dtm32", [128, 32])
    i_tril = dram("trilneg", [128, 128])
    i_wzzT = dram("wzzT", [D, DI])
    i_woutT = dram("woutT", [DI, D])
    i_ngw = dram("ngw", [128, 16])
    i_wgate = dram("wgate", [D, 8])
    i_bgate = dram("bgate", [1, 8])
    i_alpha = dram("alpha", [1, 1])
    i_w1m = dram("w1m", [D, 4096])
    i_b1m = dram("b1m", [128, 32])
    i_w2m = dram("w2m", [4096, D])
    i_b2m = dram("b2m", [128, 8])
    i_w1e = dram("w1e", [8 * D, 4096])
    i_b1e = dram("b1e", [8 * 128, 32])
    i_w2e = dram("w2e", [8 * 4096, D])
    i_b2e = dram("b2e", [8 * 128, 8])
    o_out = nc.dram_tensor("o_xT", [D, T], F32, kind="ExternalOutput").ap()
    d_rows = nc.dram_tensor("d_rows", [1, T], BF16, kind="Internal").ap()
    d_w1b = nc.dram_tensor("d_w1b", [2 * D, 4096], BF16, kind="Internal").ap()
    d_w2b = nc.dram_tensor("d_w2b", [2 * 4096, D], BF16, kind="Internal").ap()

    dbg_outs = {}
    def dbg_out(name, shape, dt):
        if name in dbg:
            dbg_outs[name] = nc.dram_tensor("dbg_" + name, shape, dt, kind="ExternalOutput").ap()
        return dbg_outs.get(name)

    db_mod = dbg_out("modcols", [128, 48], F32)
    db_q = dbg_out("q", [128, NK * T], BF16)
    db_hin = dbg_out("hin", [128, NK * T], BF16)
    db_cos = dbg_out("cos", [128, T], BF16)
    db_sin = dbg_out("sin", [128, T], BF16)
    db_xbcc = dbg_out("xbcc", [128, NJC * T], BF16)
    db_dt = dbg_out("dt", [32, T], F32)
    db_acs = dbg_out("acs", [32, T], F32)
    db_tm = dbg_out("tmdt", [128, NCH * 64], BF16)
    db_y = dbg_out("yfm", [128, 16 * T], BF16)
    db_x1 = dbg_out("x1", [128, NK * T], BF16)

    sb = nc.alloc_sbuf_tensor
    x0b = sb("x0b", [128, NK * T], BF16)
    qb = sb("qb", [128, NK * T], BF16)
    xbc = sb("xbc", [128, NJC * T], BF16)
    hbuf = sb("hbuf", [128, NK * T], BF16)
    warena = sb("warena", [128, 4608], BF16)
    id_f = sb("id_f", [128, 128], F32)
    id_b = sb("id_b", [128, 128], BF16)
    tril_b = sb("tril_b", [128, 128], BF16)
    onesb = sb("onesb", [1, 128], BF16)
    onescol = sb("onescol", [128, 1], BF16)
    smallf = sb("smallf", [128, 384], F32)
    smallb = sb("smallb", [128, 128], BF16)
    S_sb = sb("S_sb", [128, DI], BF16)
    tmdt = sb("tmdt", [128, NCH * 64], BF16)
    negacs = sb("negacs", [128, NCH * 32], F32)
    expacs = sb("expacs", [128, NCH * 32], F32)
    acs_hi = sb("acs_hi", [32, T], BF16)
    acs_lo = sb("acs_lo", [32, T], BF16)
    convw_sb = sb("convw_sb", [128, NJC * 4], F32)
    rowtf = sb("rowtf", [1, 32], F32)
    rowtb = sb("rowtb", [1, 16], BF16)
    idx8 = sb("idx8", [1, 8], U32)
    top8 = sb("top8", [1, 8], F32)
    wg_sb = sb("wg_sb", [128, 64], BF16)

    COL = {}
    _o = [0]
    for nm, n in [("mod", 48), ("npw", NK), ("onepca", NK), ("ga", NK),
                  ("onepcb", NK), ("gb", NK),
                  ("eps", 1), ("pio2", 1), ("fcol", 1), ("cj", 1), ("sj", 1),
                  ("t5", 8), ("t6", 8), ("t7", 8), ("iota32", 1),
                  ("convb", NJC), ("dtb", 1), ("nega", 1), ("neglast", NCH),
                  ("expal", NCH), ("drep", 16), ("ngw", 16), ("b1m", 32), ("b2m", 8),
                  ("b1e1", 32), ("b1e2", 32), ("b2e1", 8), ("b2e2", 8), ("bcomb", 8),
                  ("coefm", 1), ("coefe1", 1), ("coefe2", 1), ("meancol", 8)]:
        COL[nm] = _o[0]
        _o[0] += n
    assert _o[0] <= 384, _o[0]

    def c_(name, i=0, n=1, p=128):
        b = COL[name] + i
        return smallf[0:p, b:b + n]

    CB = {}
    _o[0] = 0
    for nm, n in [("ccol", NK), ("scol", NK), ("mean8", NK), ("dtm32", 32)]:
        CB[nm] = _o[0]
        _o[0] += n

    def cb_(name, i=0, n=1):
        b = CB[name] + i
        return smallb[:, b:b + n]

    cosrep = warena[:, 0:2048]
    sinrep2 = warena[:, 2048:4096]

    def wq_slab(k):
        return xbc[:, k * 1024:(k + 1) * 1024]

    mu_row = xbc[0:1, 8192:10240]
    rstd_row = xbc[0:1, 10240:12288]
    rstd2_row = xbc[0:1, 12288:14336]
    scr0 = xbc[:, 14336:14848]
    scr1 = xbc[:, 14848:15360]
    qsw = xbc[:, 15360:17408]

    def wmod_set(par, k):
        base = 17408 + par * 2048 + k * 256
        return xbc[:, base:base + 256]

    fp32scr = xbc[:, 21504:33792].bitcast(F32)
    mod_row = fp32scr[0:1, 0:6144]
    cos_f = fp32scr[0:32, 0:2048]
    sin_f = fp32scr[0:32, 2048:4096]
    tsc1 = fp32scr[0:32, 4096:5120]
    tsc2 = fp32scr[0:32, 5120:6144]
    rowscr = xbc[:, 33792:36864].bitcast(F32)
    r_mu = rowscr[0:1, 0:512]
    r_msq = rowscr[0:1, 512:1024]
    r_var = rowscr[0:1, 1024:1536]

    def wz_slab(par, k):
        base = par * 2048 + k * 256
        return warena[:, base:base + 256]

    dtraw = S_sb[0:32, 0:T]
    h32 = hbuf[:, :].bitcast(F32)
    dt_fm = h32[0:32, 0:2048]
    acs_fm = h32[32:64, 0:2048]
    dec_fm = h32[64:96, 0:2048]
    adt_fm = h32[0:32, 2048:4096]
    dps1 = h32[0:32, 4096:6144]
    dps2 = h32[0:32, 6144:8192]
    stack3 = h32[0:96, 0:2048]
    convdiag = [hbuf[:, 12288 + i * 128: 12288 + (i + 1) * 128] for i in range(4)]
    halo = [hbuf[:, 12800:12803], hbuf[:, 12803:12806]]
    drow_stage = hbuf[0:16, 8960:8992]
    drow = hbuf[0:1, 8448:8960]
    x_tm = hbuf[:, 0:2048]
    xdtd_t = hbuf[:, 2048:4096]
    Y_tm = hbuf[:, 4096:6144]
    expss = [hbuf[:, 6144:6656], hbuf[:, 6656:7168]]
    Mh = [hbuf[:, 12288 + i * 512: 12288 + (i + 1) * 512] for i in range(4)]
    cbt = hbuf[:, 8192:8320]
    Br_tm = hbuf[:, 8320:8448]
    xdt_tm = hbuf[:, 8960:11008]
    tmp_tm = hbuf[:, 14336:16384]

    ctx = ExitStack()
    sm = Sems(nc, ctx, SEM_NAMES)
    snap = {}

    with ExitStack() as pst, nc.Block() as blk:
        p_mod = [pst.enter_context(nc.psum_tensor(f"p_mod{i}", [1, 256], F32)) for i in range(2)]
        p_modT = pst.enter_context(nc.psum_tensor("p_modT", [128, 48], F32))
        p_sum = [pst.enter_context(nc.psum_tensor(f"p_sum{i}", [1, 512], F32)) for i in range(2)]
        p_sq = [pst.enter_context(nc.psum_tensor(f"p_sq{i}", [1, 512], F32)) for i in range(2)]

        @blk.gpsimd
        def _(g):
            g = EG(g)
            sm.inc(g.dma_start(id_b[:, :], i_identf), "cD", 16)
            sm.inc(g.dma_start(tril_b[:, :], i_tril), "cD", 16)
            sm.inc(g.dma_start(cb_("ccol", 0, NK), i_ccol), "cD", 16)
            for k in range(NK):
                sm.inc(g.dma_start(x0b[:, k * T:(k + 1) * T],
                                   i_xT[k * 128:(k + 1) * 128, :]), "x0D", 16)
            sm.inc(g.dma_start(xbc[:, 0:8192].rearrange("p (k c) -> p k c", k=NK),
                               bass.AP(i_wqT.tensor, 0,
                                       [[1024, 128], [128 * 1024, NK], [1, 1024]])),
                   "qD", 16)

        @blk.sync
        def _(sy):
            sm.inc(sy.dma_start(id_f[:, :], i_identf), "xD", 16)
            sm.inc(sy.dma_start(c_("npw", 0, NK), i_npw), "xD", 16)
            sm.inc(sy.dma_start(c_("iota32", p=H), i_iota32), "xD", 16)
            sm.inc(sy.dma_start(c_("ngw", 0, 16), i_ngw), "xD", 16)
            snap["xD_consts"] = sm.snap("xD")
            qb32 = qb[:, :].bitcast(F32)
            wsem = ["wmA", "wmB", "wmC", "wmD"]
            for jg in range(24):
                if jg >= 4:
                    sy.wait_ge(sm.h["mcst"], jg - 3)
                s4 = jg % 4
                dst = qb32[:, s4 * 2048:(s4 + 1) * 2048].rearrange("p (k c) -> p k c", k=NK)
                sm.inc(sy.dma_start(dst,
                                    bass.AP(i_wmod.tensor, jg * 256,
                                            [[6144, 128], [128 * 6144, NK], [1, 256]])),
                       wsem[s4], 16)

        @blk.vector
        def _(v):
            v = EG(v)
            v.memset(c_("eps"), EPS)
            v.memset(c_("pio2"), math.pi / 2)
            v.memset(onesb[:, :], 1.0)
            v.memset(onescol[:, :], 1.0)
            sm.inc(v.drain(), "misc", 1)
            v.wait_ge(sm.h["x0D"], 16 * NK)
            idx = 0
            for tt in range(4):
                for k in range(NK):
                    if idx >= 2:
                        v.wait_ge(sm.h["sqm"], idx - 1)
                    s_ = x0b[:, k * T + tt * 512: k * T + (tt + 1) * 512]
                    sm.inc(v.tensor_mul(scr0 if idx % 2 == 0 else scr1, s_, s_), "sqv", 1)
                    idx += 1
                v.wait_ge(sm.h["sqm"], 8 * (tt + 1))
                ps, pq = p_sum[tt % 2], p_sq[tt % 2]
                v.tensor_scalar(r_mu, ps[:, :], 1.0 / D, None, ALU.mult)
                v.tensor_copy(mu_row[:, tt * 512:(tt + 1) * 512], r_mu)
                v.tensor_scalar(r_msq, pq[:, :], 1.0 / D, None, ALU.mult)
                v.tensor_mul(r_var, r_mu, r_mu)
                v.tensor_sub(r_var, r_msq, r_var)
                sm.inc(v.drain(), "rowv", 1)
                v.wait_ge(sm.h["rows"], tt + 1)
                v.tensor_copy(rstd_row[:, tt * 512:(tt + 1) * 512], r_msq)
                sm.inc(v.drain(), "rowv", 1)
            v.wait_ge(sm.h["mmod"], 25)
            sm.inc(v.tensor_copy(c_("mod", 0, 48), p_modT[:, :]), "mev", 1)

        @blk.scalar
        def _(s):
            s = EG(s)
            s.wait_ge(sm.h["cD"], 48)
            sm.inc(s.activation(cb_("scol", 0, NK), cb_("ccol", 0, NK), AF.Silu),
                   "misc", 1)
            s.wait_ge(sm.h["xD"], snap["xD_consts"])
            sm.inc(s.activation(c_("fcol", p=H), c_("iota32", p=H), AF.Exp,
                                scale=-math.log(10000.0) / 32.0), "misc", 1)
            qb32s = qb[:, :].bitcast(F32)
            wsem = ["wmA", "wmB", "wmC", "wmD"]
            for jg in range(24):
                s4 = jg % 4
                s.wait_ge(sm.h[wsem[s4]], 16 * (jg // 4 + 1))
                if jg >= 2:
                    s.wait_ge(sm.h["mmod"], jg - 1)
                sm.inc(s.activation(xbc[:, 17408 + (jg % 2) * 2048: 17408 + (jg % 2) * 2048 + 2048],
                                    qb32s[:, s4 * 2048:(s4 + 1) * 2048], AF.Copy),
                       "mcst", 1)
                if jg >= 2:
                    s.wait_ge(sm.h["mmod"], jg - 1)
                    sm.inc(s.copy(mod_row[:, (jg - 2) * 256:(jg - 1) * 256],
                                  p_mod[jg % 2][:, :]), "mev", 1)
            for jg in (22, 23):
                s.wait_ge(sm.h["mmod"], jg + 1)
                sm.inc(s.copy(mod_row[:, jg * 256:(jg + 1) * 256], p_mod[jg % 2][:, :]), "mev", 1)
            for tt in range(4):
                s.wait_ge(sm.h["rowv"], 2 * tt + 1)
                s.activation(r_msq, r_var, AF.Ln, bias=c_("eps", p=1))
                sm.inc(s.activation(r_msq, r_msq, AF.Exp, scale=-0.5), "rows", 1)

        @blk.tensor
        def _(t):
            t.wait_ge(sm.h["misc"], 2)
            for jg in range(24):
                t.wait_ge(sm.h["mcst"], jg + 1)
                if jg >= 2:
                    t.wait_ge(sm.h["mev"], jg - 1)
                for k in range(NK):
                    mm = t.matmul(p_mod[jg % 2][:, :], cb_("scol", k, 1), wmod_set(jg % 2, k),
                                  start=(k == 0), stop=(k == NK - 1))
                sm.inc(mm, "mmod", 1)
            t.wait_ge(sm.h["mev"], 24)
            t.wait_ge(sm.h["xD"], snap["xD_consts"])
            for i in range(48):
                t.transpose(p_modT[:, i:i + 1], mod_row[:, i * 128:(i + 1) * 128], id_f[0:1, 0:1])
            sm.inc(t.nop(), "mmod", 1)
            t.wait_ge(sm.h["x0D"], 16 * NK)
            idx = 0
            for tt in range(4):
                if tt >= 2:
                    t.wait_ge(sm.h["rowv"], 2 * (tt - 2) + 2)
                for k in range(NK):
                    t.matmul(p_sum[tt % 2][:, :], onescol[:, :],
                             x0b[:, k * T + tt * 512: k * T + (tt + 1) * 512],
                             start=(k == 0), stop=(k == NK - 1))
                    t.wait_ge(sm.h["sqv"], idx + 1)
                    mm = t.matmul(p_sq[tt % 2][:, :], onescol[:, :],
                                  scr0 if idx % 2 == 0 else scr1,
                                  start=(k == 0), stop=(k == NK - 1))
                    sm.inc(mm, "sqm", 1)
                    idx += 1

    with ExitStack() as pst, nc.Block() as blk:
        p_repA = [pst.enter_context(nc.psum_tensor(f"p_repA{i}", [128, 512], F32)) for i in range(2)]
        p_repB = [pst.enter_context(nc.psum_tensor(f"p_repB{i}", [128, 512], F32)) for i in range(2)]
        rowvB = sm.snap("rowv")
        rowsB = sm.snap("rows")
        qmmB = sm.snap("qmm")
        sqvB = sm.snap("sqv")
        hvB = sm.snap("hv")
        miscB = sm.snap("misc")

        @blk.vector
        def _(v):
            v = EG(v)
            v.tensor_scalar(c_("onepca", 0, NK), c_("mod", 8, NK), 1.0, None, ALU.add)
            v.tensor_scalar(c_("onepcb", 0, NK), c_("mod", 32, NK), 1.0, None, ALU.add)
            for gi, (srcc, dstn) in enumerate(((16, "ga"), (40, "gb"))):
                g8 = c_("mod", srcc, NK)
                v.tensor_scalar(c_("t5", 0, NK), g8, 0.0, None, ALU.is_ge)
                v.tensor_scalar(c_("t5", 0, NK), c_("t5", 0, NK), 2.0, -1.0, ALU.mult, ALU.add)
                v.wait_ge(sm.h["rows"], rowsB + 3 + gi)
                absb = c_("t6", 0, NK) if gi == 0 else c_("t7", 0, NK)
                v.tensor_scalar(absb, absb, 0.25, None, ALU.max)
                v.tensor_mul(c_(dstn, 0, NK), c_("t5", 0, NK), absb)
            v.drain()
            sm.inc(v.drain(), "misc", 1)
            v.memset(cos_f[:, 0:1], 1.0)
            v.memset(sin_f[:, 0:1], 0.0)
            v.wait_ge(sm.h["rows"], rowsB + 2)
            cj = c_("cj", p=32)
            sjc = c_("sj", p=32)
            for lev in range(11):
                w = 1 << lev
                v.tensor_scalar(tsc1[:, 0:w], cos_f[:, 0:w], cj, None, ALU.mult)
                v.tensor_scalar(tsc2[:, 0:w], sin_f[:, 0:w], sjc, None, ALU.mult)
                v.tensor_sub(cos_f[:, w:2 * w], tsc1[:, 0:w], tsc2[:, 0:w])
                v.tensor_scalar(tsc1[:, 0:w], sin_f[:, 0:w], cj, None, ALU.mult)
                v.tensor_scalar(tsc2[:, 0:w], cos_f[:, 0:w], sjc, None, ALU.mult)
                v.tensor_add(sin_f[:, w:2 * w], tsc1[:, 0:w], tsc2[:, 0:w])
                if lev < 10:
                    v.tensor_mul(c_("t5", p=32), cj, cj)
                    v.tensor_mul(c_("t6", p=32), sjc, sjc)
                    v.tensor_mul(c_("t7", p=32), cj, sjc)
                    v.tensor_sub(cj, c_("t5", p=32), c_("t6", p=32))
                    v.tensor_add(sjc, c_("t7", p=32), c_("t7", p=32))
                    v.drain()
            for r in range(4):
                v.tensor_copy(cosrep[r * 32:(r + 1) * 32, :], cos_f[:, :])
                if r % 2 == 0:
                    v.tensor_scalar(sinrep2[r * 32:(r + 1) * 32, :], sin_f[:, :], -1.0, None, ALU.mult)
                else:
                    v.tensor_copy(sinrep2[r * 32:(r + 1) * 32, :], sin_f[:, :])
            sm.inc(v.drain(), "misc", 1)
            for tt in range(4):
                v.wait_ge(sm.h["qmm"], qmmB + tt + 1)
                for k in range(NK):
                    xs = x0b[:, k * T + tt * 512: k * T + (tt + 1) * 512]
                    v.tensor_sub(scr0, xs, p_repA[tt % 2][:, :])
                    v.tensor_mul(scr0, scr0, p_repB[tt % 2][:, :])
                    v.tensor_scalar(hbuf[:, k * T + tt * 512: k * T + (tt + 1) * 512],
                                    scr0, c_("onepca", k), c_("mod", 0 + k),
                                    ALU.mult, ALU.add)
                    idx_unused = 0
                sm.inc(v.drain(), "rowv", 1)

        @blk.scalar
        def _(s):
            s = EG(s)
            sm.inc(s.activation(c_("cj", p=32), c_("fcol", p=32), AF.Sin,
                                scale=-1.0, bias=c_("pio2", p=32)), "rows", 1)
            sm.inc(s.activation(c_("sj", p=32), c_("fcol", p=32), AF.Sin), "rows", 1)
            sm.inc(s.activation(c_("t6", 0, NK), c_("mod", 16, NK), AF.Abs), "rows", 1)
            sm.inc(s.activation(c_("t7", 0, NK), c_("mod", 40, NK), AF.Abs), "rows", 1)

        @blk.tensor
        def _(t):
            for tt in range(4):
                if tt >= 2:
                    t.wait_ge(sm.h["rowv"], rowvB + (tt - 2) + 1)
                t.matmul(p_repA[tt % 2][:, :], onesb[:, :],
                         mu_row[:, tt * 512:(tt + 1) * 512], start=True, stop=True)
                sm.inc(t.matmul(p_repB[tt % 2][:, :], onesb[:, :],
                                rstd_row[:, tt * 512:(tt + 1) * 512], start=True, stop=True),
                       "qmm", 1)

    with ExitStack() as pst, nc.Block() as blk:
        p_q = [pst.enter_context(nc.psum_tensor(f"p_q{i}", [128, 512], F32)) for i in range(2)]
        p_s2 = [pst.enter_context(nc.psum_tensor(f"p_s2{i}", [1, 512], F32)) for i in range(2)]
        p_rep2 = [pst.enter_context(nc.psum_tensor(f"p_rep2{i}", [128, 512], F32)) for i in range(2)]
        qmm0 = sm.snap("qmm")
        qev0 = sm.snap("qev")
        sqv0 = sm.snap("sqv")
        sqm0 = sm.snap("sqm")
        stm0 = sm.snap("stm")
        rowv0 = sm.snap("rowv")
        rows0 = sm.snap("rows")
        hv0 = sm.snap("hv")

        @blk.tensor
        def _(t):
            t.wait_ge(sm.h["qD"], 16)
            idx = 0
            for j in range(NK):
                for tt in range(4):
                    if j == 0:
                        t.wait_ge(sm.h["rowv"], rowv0 - 3 + tt)
                    if idx >= 2:
                        t.wait_ge(sm.h["qev"], qev0 + idx - 1)
                    for k in range(NK):
                        mm = t.matmul(p_q[idx % 2][:, :],
                                      wq_slab(k)[:, j * 128:(j + 1) * 128],
                                      hbuf[:, k * T + tt * 512: k * T + (tt + 1) * 512],
                                      start=(k == 0), stop=(k == NK - 1))
                    sm.inc(mm, "qmm", 1)
                    idx += 1
            idx = 0
            for tt in range(4):
                if tt >= 2:
                    t.wait_ge(sm.h["rowv"], rowv0 + 2 * (tt - 2) + 1)
                for k in range(NK):
                    t.wait_ge(sm.h["sqv"], sqv0 + idx + 1)
                    mm = t.matmul(p_s2[tt % 2][:, :], onescol[:, :],
                                  scr0 if idx % 2 == 0 else scr1,
                                  start=(k == 0), stop=(k == NK - 1))
                    sm.inc(mm, "sqm", 1)
                    idx += 1
            for tt in range(4):
                t.wait_ge(sm.h["rowv"], rowv0 + 2 * tt + 2)
                if tt >= 2:
                    t.wait_ge(sm.h["sqv"], sqv0 + 32 + 8 * (tt - 2) + 8)
                sm.inc(t.matmul(p_rep2[tt % 2][:, :], onesb[:, :],
                                rstd2_row[:, tt * 512:(tt + 1) * 512], start=True, stop=True),
                       "qmm", 1)

        @blk.vector
        def _(v):
            v = EG(v)
            for idx in range(32):
                v.wait_ge(sm.h["qmm"], qmm0 + idx + 1)
                j, tt = divmod(idx, 4)
                sm.inc(v.tensor_copy(qb[:, j * T + tt * 512: j * T + (tt + 1) * 512],
                                     p_q[idx % 2][:, :]), "qev", 1)
            for j in range(NK):
                qs = qb[:, j * T:(j + 1) * T]
                v.tensor_copy(qsw[0:32, :], qs[32:64, :])
                v.tensor_copy(qsw[32:64, :], qs[0:32, :])
                v.tensor_copy(qsw[64:96, :], qs[96:128, :])
                v.tensor_copy(qsw[96:128, :], qs[64:96, :])
                v.tensor_mul(qsw[:, :], qsw[:, :], sinrep2[:, :])
                v.tensor_mul(qs[:, :], qs[:, :], cosrep[:, :])
                v.tensor_add(qs[:, :], qs[:, :], qsw[:, :])
            sm.inc(v.drain(), "misc", 1)
            idx = 0
            for tt in range(4):
                for k in range(NK):
                    if idx >= 2:
                        v.wait_ge(sm.h["sqm"], sqm0 + idx - 1)
                    qs = qb[:, k * T + tt * 512: k * T + (tt + 1) * 512]
                    sm.inc(v.tensor_mul(scr0 if idx % 2 == 0 else scr1, qs, qs), "sqv", 1)
                    idx += 1
                v.wait_ge(sm.h["sqm"], sqm0 + 8 * (tt + 1))
                v.tensor_scalar(r_msq, p_s2[tt % 2][:, :], 1.0 / D, EPS, ALU.mult, ALU.add)
                sm.inc(v.drain(), "rowv", 1)
                v.wait_ge(sm.h["rows"], rows0 + tt + 1)
                v.tensor_copy(rstd2_row[:, tt * 512:(tt + 1) * 512], r_var)
                sm.inc(v.drain(), "rowv", 1)
            idx = 0
            for tt in range(4):
                v.wait_ge(sm.h["qmm"], qmm0 + 32 + tt + 1)
                for k in range(NK):
                    if idx < 2:
                        v.wait_ge(sm.h["sqm"], sqm0 + 30 + idx + 1)
                    else:
                        v.wait_ge(sm.h["hv"], hv0 + idx - 1)
                    qs = qb[:, k * T + tt * 512: k * T + (tt + 1) * 512]
                    sm.inc(v.tensor_mul(scr0 if idx % 2 == 0 else scr1, qs,
                                        p_rep2[tt % 2][:, :]), "sqv", 1)
                    idx += 1

        @blk.scalar
        def _(s):
            s = EG(s)
            for tt in range(4):
                s.wait_ge(sm.h["rowv"], rowv0 + 2 * tt + 1)
                s.activation(r_var, r_msq, AF.Ln)
                sm.inc(s.activation(r_var, r_var, AF.Exp, scale=-0.5), "rows", 1)
            idx = 0
            for tt in range(4):
                for k in range(NK):
                    s.wait_ge(sm.h["sqv"], sqv0 + 32 + idx + 1)
                    sm.inc(s.activation(hbuf[:, k * T + tt * 512: k * T + (tt + 1) * 512],
                                        scr0 if idx % 2 == 0 else scr1, AF.Copy,
                                        scale=c_("npw", k)), "hv", 1)
                    idx += 1


        @blk.sync
        def _(sy):
            sy.wait_ge(sm.h["rowv"], rowv0 + 8)
            sm.inc(sy.dma_start(d_rows, rstd2_row), "st", 16)

    jlist = [(j, 128) for j in range(NJC)] + [(NJC, 32)]
    groups = [jlist[i:i + 2] for i in range(0, len(jlist), 2)]
    evcum = []
    tot = 0
    for grp in groups:
        tot += len(grp) * 4
        evcum.append(tot)
    pstD = ExitStack()
    blkD = nc.Block()
    blkD.__enter__()
    if True:
        blk = blkD
        p_z = [pstD.enter_context(nc.psum_tensor(f"p_z{i}", [128, 512], F32)) for i in range(8)]
        stD0 = sm.snap("st")

        @blk.gpsimd
        def _(g):
            g = EG(g)
            for gi, grp in enumerate(groups):
                if gi >= 2:
                    g.wait_ge(sm.h["zev"], evcum[gi - 2])
                par = gi % 2
                c0 = grp[0][0] * 128
                w = sum(x[1] for x in grp)
                for k in range(NK):
                    sm.inc(g.dma_start(wz_slab(par, k)[:, 0:w],
                                       i_wzxcT[k * 128:(k + 1) * 128, c0:c0 + w]),
                           "wzA" if par == 0 else "wzB", 16)
            sm.inc(g.dma_start(convw_sb[:, :], i_convw), "x2D", 16)
            sm.inc(g.dma_start(c_("convb", 0, NJC), i_convb), "x2D", 16)
            sm.inc(g.dma_start(c_("dtb", p=H), i_dtb), "x2D", 16)
            sm.inc(g.dma_start(c_("nega", p=H), i_alog), "x2D", 16)
            sm.inc(g.dma_start(c_("drep", 0, 16), i_drep), "x2D", 16)
            sm.inc(g.dma_start(cb_("dtm32", 0, 32), i_dtm32), "x2D", 16)

        @blk.tensor
        def _(t):
            for gi, grp in enumerate(groups):
                par = gi % 2
                t.wait_ge(sm.h["wzA" if par == 0 else "wzB"], 16 * NK * (gi // 2 + 1))
                if gi >= 1:
                    t.wait_ge(sm.h["zev"], evcum[gi - 1])
                for jj, (j, w) in enumerate(grp):
                    for tt in range(4):
                        bank = jj * 4 + tt
                        for k in range(NK):
                            mm = t.matmul(p_z[bank][0:w, :],
                                          wz_slab(par, k)[:, jj * 128: jj * 128 + w],
                                          hbuf[:, k * T + tt * 512:k * T + (tt + 1) * 512],
                                          start=(k == 0), stop=(k == NK - 1))
                        sm.inc(mm, "zmm", 1)

        @blk.vector
        def _(v):
            v = EG(v)
            v.wait_ge(sm.h["st"], stD0)
            nz = 0
            for gi, grp in enumerate(groups):
                for jj, (j, w) in enumerate(grp):
                    for tt in range(4):
                        bank = jj * 4 + tt
                        nz += 1
                        v.wait_ge(sm.h["zmm"], nz)
                        if j < NJC:
                            dst = xbc[:, j * T + tt * 512: j * T + (tt + 1) * 512]
                        else:
                            dst = dtraw[:, tt * 512:(tt + 1) * 512]
                        sm.inc(v.tensor_copy(dst, p_z[bank][0:w, :]), "zev", 1)

    if True:
        p_dtt = p_z[0][:, 0:96]
        p_drw = p_z[1][0:16, 0:32]
        dtv0 = sm.snap("dtv")
        dts0 = sm.snap("dts")
        dtm0 = sm.snap("dtm")

        @blk.vector
        def _(v):
            v = EG(v)
            v.drain()
            v.wait_ge(sm.h["x2D"], sm.c["x2D"])
            v.tensor_scalar(dps1, dtraw, c_("dtb", p=32), None, ALU.add)
            sm.inc(v.drain(), "dtv", 1)
            v.tensor_scalar(dps2, dps1, 0.0, None, ALU.max)
            v.wait_ge(sm.h["dts"], dts0 + 1)
            v.tensor_add(dt_fm, dt_fm, dps2)
            v.wait_ge(sm.h["dts"], dts0 + 2)
            v.tensor_scalar(c_("nega", p=32), c_("nega", p=32), -1.0, None, ALU.mult)
            v.drain()
            v.tensor_scalar(adt_fm, dt_fm, c_("nega", p=32), None, ALU.mult)
            v.drain()
            for cc in range(NCH):
                v.tensor_tensor_scan(acs_fm[:, cc * 128:(cc + 1) * 128],
                                     adt_fm[:, cc * 128:(cc + 1) * 128],
                                     adt_fm[:, cc * 128:(cc + 1) * 128],
                                     0.0, ALU.add, ALU.bypass)
            v.tensor_copy(smallf[32:64, COL["neglast"]:COL["neglast"] + NCH],
                          acs_fm[:, 127::128])
            sm.inc(v.drain(), "dtv", 1)
            v.tensor_copy(dps2, acs_fm)
            v.drain()
            v.tensor_copy(acs_hi[:, :], dps2)
            v.drain()
            v.tensor_tensor(acs_lo[:, :], dps2, acs_hi[:, :], ALU.subtract)
            v.wait_ge(sm.h["dts"], dts0 + 3 + NCH)
            sm.inc(v.drain(), "dtv", 1)
            for cc in range(NCH):
                v.wait_ge(sm.h["dtm"], dtm0 + cc + 1)
                v.tensor_copy(tmdt[:, cc * 64: cc * 64 + 32], p_dtt[:, 0:32])
                v.tensor_copy(tmdt[:, cc * 64 + 32: cc * 64 + 64], p_dtt[:, 64:96])
                sm.inc(v.tensor_scalar(negacs[:, cc * 32:(cc + 1) * 32], p_dtt[:, 32:64],
                                       -1.0, None, ALU.mult), "dtv", 1)
            v.wait_ge(sm.h["dtm"], dtm0 + NCH + 1)
            sm.inc(v.tensor_copy(drow_stage, p_drw[:, :]), "dtv", 1)

        @blk.scalar
        def _(s):
            s = EG(s)
            s.wait_ge(sm.h["dtv"], dtv0 + 1)
            s.activation(dt_fm, dps1, AF.Abs)
            s.activation(dt_fm, dt_fm, AF.Exp, scale=-1.0)
            sm.inc(s.activation(dt_fm, dt_fm, AF.Ln, bias=1.0), "dts", 1)
            s.wait_ge(sm.h["x2D"], sm.c["x2D"])
            sm.inc(s.activation(c_("nega", p=32), c_("nega", p=32), AF.Exp), "dts", 1)
            s.wait_ge(sm.h["dtv"], dtv0 + 2)
            sm.inc(s.activation(c_("expal", 0, NCH, p=32),
                                smallf[32:64, COL["neglast"]:COL["neglast"] + NCH],
                                AF.Exp), "dts", 1)
            for cc in range(NCH):
                sm.inc(s.activation(dec_fm[:, cc * 128:(cc + 1) * 128],
                                    acs_fm[:, cc * 128:(cc + 1) * 128],
                                    AF.Exp, scale=-1.0,
                                    bias=smallf[32:64, COL["neglast"] + cc:COL["neglast"] + cc + 1]),
                       "dts", 1)
            for cc in range(NCH):
                s.wait_ge(sm.h["dtv"], dtv0 + 3 + cc + 1)
                sm.inc(s.activation(expacs[:, cc * 32:(cc + 1) * 32],
                                    negacs[:, cc * 32:(cc + 1) * 32],
                                    AF.Exp, scale=-1.0), "dts", 1)

        @blk.tensor
        def _(t):
            t.wait_ge(sm.h["dtv"], dtv0 + 3)
            t.wait_ge(sm.h["dts"], dts0 + 3 + NCH)
            for cc in range(NCH):
                if cc >= 1:
                    t.wait_ge(sm.h["dtv"], dtv0 + 3 + cc)
                sm.inc(t.transpose(p_dtt[:, :], stack3[:, cc * 128:(cc + 1) * 128],
                                   id_f[0:96, 0:96]), "dtm", 1)
            t.wait_ge(sm.h["dts"], dts0 + 3)
            sm.inc(t.transpose(p_drw[:, :], c_("expal", 0, NCH, p=32), id_f[0:32, 0:32]),
                   "dtm", 1)

    if True:
        p_cv = [p_z[2], p_z[3]]
        cvm0 = sm.snap("cvm")
        cvv0 = sm.snap("cvv")
        cvs0 = sm.snap("cvs")

        @blk.gpsimd
        def _(g):
            g.wait_ge(sm.h["dtv"], sm.c["dtv"])
            sm.inc(g.dma_start(drow, drow_stage), "x2D", 16)

        @blk.vector
        def _(v):
            v = EG(v)
            v.drain()
            for j in range(NJC):
                if j >= 1:
                    v.wait_ge(sm.h["cvm"], cvm0 + 4 * j)
                for k in range(4):
                    v.tensor_scalar(convdiag[k], id_b[:, :],
                                    convw_sb[:, j * 4 + k:j * 4 + k + 1], None, ALU.mult)
                sm.inc(v.drain(), "cvv", 1)
                for tt in range(3):
                    v.wait_ge(sm.h["cvm"], cvm0 + 4 * j + tt)
                    sm.inc(v.tensor_copy(halo[tt % 2],
                                         xbc[:, j * T + (tt + 1) * 512 - 3: j * T + (tt + 1) * 512]),
                           "cvv", 1)

        @blk.tensor
        def _(t):
            idx = 0
            for j in range(NJC):
                for tt in range(4):
                    t.wait_ge(sm.h["cvv"], cvv0 + 4 * j + 1 + min(tt, 3))
                    if idx >= 2:
                        t.wait_ge(sm.h["cvs"], cvs0 + idx - 1)
                    bank = p_cv[idx % 2]
                    base = j * T + tt * 512
                    t.matmul(bank[:, :], convdiag[3], xbc[:, base: base + 512],
                             start=True, stop=False)
                    for k in range(3):
                        d = k - 3
                        if tt > 0:
                            t.matmul(bank[:, 0:-d], convdiag[k],
                                     halo[(tt - 1) % 2][:, 3 + d: 3],
                                     start=False, stop=False)
                        mm = t.matmul(bank[:, -d:512], convdiag[k],
                                      xbc[:, base: base + 512 + d],
                                      start=False, stop=(k == 2))
                    sm.inc(mm, "cvm", 1)
                    idx += 1

        @blk.scalar
        def _(s):
            s = EG(s)
            idx = 0
            for j in range(NJC):
                for tt in range(4):
                    idx += 1
                    s.wait_ge(sm.h["cvm"], cvm0 + idx)
                    s.wait_ge(sm.h["cvv"], cvv0 + 4 * j + 1 + min(tt + 1, 3))
                    bank = p_cv[(idx - 1) % 2]
                    sm.inc(s.activation(xbc[:, j * T + tt * 512: j * T + (tt + 1) * 512],
                                        bank[:, :], AF.Silu, bias=c_("convb", j)), "cvs", 1)

    pstD.close()
    blkD.__exit__(None, None, None)

    with ExitStack() as pst, nc.Block() as blk:
        p_trx = [pst.enter_context(nc.psum_tensor(f"p_tr{i}", [128, 1024], BF16))
                 for i in range(2)]
        p_ss = pst.enter_context(nc.psum_tensor("p_ss", [128, 512], F32))
        p_yd = [pst.enter_context(nc.psum_tensor(f"p_yd{i}", [128, 512], F32)) for i in range(2)]
        p_yo = [pst.enter_context(nc.psum_tensor(f"p_yo{i}", [128, 512], F32)) for i in range(2)]
        p_st = p_yo[1][:, :]
        p_dr = p_yo[0][:, :]

        g0 = {n: sm.snap(n) for n in ("g1", "g2", "g2s", "g3", "g4", "g5", "g6", "g7",
                                      "g8", "g9", "g10", "g12", "g13")}
        zev0G = sm.snap("zev")
        dts_all = sm.c["dts"]
        x2D_all = sm.c["x2D"]
        ydiag_rec = {}
        zz = [0]

        def slotof(n):
            return p_trx[n % 2], 0

        @blk.tensor
        def _(t):
            t.wait_ge(sm.h["x2D"], x2D_all)
            for c in range(NCH):
                for k in range(5):
                    n = 9 * c + k
                    if k >= 2:
                        t.wait_ge(sm.h["g2s"], g0["g2s"] + 5 * c + k - 1)
                    elif c > 0:
                        t.wait_ge(sm.h["g2"], g0["g2"] + 4 * c - 1 + k)
                    bank, cb0 = slotof(n)
                    if k < 4:
                        for j in range(4):
                            mm = t.transpose(bank[:, cb0 + j * 128: cb0 + (j + 1) * 128],
                                             xbc[:, (4 * k + j) * T + c * 128:
                                                 (4 * k + j) * T + (c + 1) * 128],
                                             id_b[:, :])
                    else:
                        mm = t.transpose(bank[:, cb0: cb0 + 128],
                                         xbc[:, 16 * T + c * 128: 16 * T + (c + 1) * 128],
                                         id_b[:, :])
                    sm.inc(mm, "g1", 1)
                if c > 0:
                    t.wait_ge(sm.h["g10"], g0["g10"] + 8 * c)
                sm.inc(t.matmul(p_ss[:, 0:128],
                                xbc[:, 16 * T + c * 128: 16 * T + (c + 1) * 128],
                                xbc[:, 17 * T + c * 128: 17 * T + (c + 1) * 128],
                                start=True, stop=True), "g1", 1)
                for half in range(2):
                    hh0 = half * 16
                    if c > 0:
                        t.wait_ge(sm.h["g6"], g0["g6"] + 4 * c + 2 * half)
                        for a in range(2):
                            sm.inc(t.matmul(p_yo[a][:, :],
                                            xbc[:, 17 * T + c * 128: 17 * T + (c + 1) * 128],
                                            S_sb[:, (hh0 + 8 * a) * 64: (hh0 + 8 * a) * 64 + 512],
                                            start=True, stop=True), "g4", 1)
                            zz[0] += 1
                    for hg4 in range(4):
                        gidx = 8 * c + 4 * half + hg4
                        if gidx > 0:
                            t.wait_ge(sm.h["g10"], g0["g10"] + gidx)
                        t.wait_ge(sm.h["zev"], zev0G + c + 1)
                        for hi in range(4):
                            h = hh0 + hg4 * 4 + hi
                            sl = p_ss[:, hi * 128:(hi + 1) * 128]
                            lhsT_hot = id_b[0:32, h:h + 1].broadcast_to([32, 128])
                            t.matmul(sl, lhsT_hot, acs_hi[:, c * 128:(c + 1) * 128],
                                     start=True, stop=False)
                            t.matmul(sl, lhsT_hot, acs_lo[:, c * 128:(c + 1) * 128],
                                     start=False, stop=False)
                            mmm = t.matmul(sl, id_b[:, :], tril_b[:, :],
                                           start=False, stop=True)
                        sm.inc(mmm, "g3", 1)
                    t.wait_ge(sm.h["g9"], g0["g9"] + c + 1)
                    for a in range(2):
                        if half == 1:
                            t.wait_ge(sm.h["g7"], g0["g7"] + 6 * c + (a + 1))
                        elif c > 0:
                            t.wait_ge(sm.h["g7"], g0["g7"] + 6 * (c - 1) + 3 + (a + 1))
                        for hg4c in range(2):
                            grp = 4 * half + 2 * a + hg4c
                            g = 8 * c + grp
                            t.wait_ge(sm.h["g5"], g0["g5"] + g + 1)
                            for hi in range(4):
                                h = hh0 + 8 * a + 4 * hg4c + hi
                                mmd = t.matmul(p_yd[a][:, (h % 8) * 64:(h % 8) * 64 + 64],
                                               Mh[g % 4][:, (h % 4) * 128: (h % 4 + 1) * 128],
                                               xdt_tm[:, h * 64: h * 64 + 64],
                                               start=True, stop=True)
                            sm.inc(mmd, "g4", 1)
                            zz[0] += 1
                            ydiag_rec[g] = g0["g4"] + zz[0]
                    t.wait_ge(sm.h["g7"], g0["g7"] + 6 * c + 3 * half + 2)
                    t.wait_ge(sm.h["g2s"], g0["g2s"] + 5 * c + 5)
                    for a in range(2):
                        h0 = hh0 + 8 * a
                        ghg8 = 4 * c + 2 * half + a
                        if ghg8 > 0:
                            t.wait_ge(sm.h["g6"], g0["g6"] + ghg8)
                        sm.inc(t.matmul(p_st, Br_tm[:, :],
                                        xdtd_t[:, h0 * 64: h0 * 64 + 512],
                                        start=True, stop=True), "g12", 1)
                        dr_ap = bass.AP(drow.tensor, drow.offset + c * 32 + h0,
                                        [list(drow.ap[0]), [1, 8], [0, 64]])
                        sm.inc(t.matmul(p_dr, onesb[:, :], dr_ap,
                                        start=True, stop=True), "g13", 1)
                for b in range(4):
                    n = 9 * c + 5 + b
                    if b < 2:
                        t.wait_ge(sm.h["g2s"], g0["g2s"] + 5 * c + 4 + b)
                    else:
                        t.wait_ge(sm.h["g2"], g0["g2"] + 4 * c + b - 1)
                    t.wait_ge(sm.h["g7"], g0["g7"] + 6 * c + 3 * (1 if b < 2 else 2))
                    bank, cb0 = slotof(n)
                    for j in range(4):
                        mm = t.transpose(bank[:, cb0 + j * 128: cb0 + (j + 1) * 128],
                                         Y_tm[:, (4 * b + j) * 128:(4 * b + j + 1) * 128],
                                         id_b[:, :])
                    sm.inc(mm, "g8", 1)

        @blk.vector
        def _(v):
            v = EG(v)
            v.memset(S_sb[:, :], 0.0)
            v.wait_ge(sm.h["dts"], dts_all)
            v.wait_ge(sm.h["x2D"], x2D_all)
            for c in range(NCH):
                v.wait_ge(sm.h["g1"], g0["g1"] + 6 * c + 6)
                sm.inc(v.tensor_copy(cbt[:, :], p_ss[:, 0:128]), "zev", 1)
                v.wait_ge(sm.h["g2s"], g0["g2s"] + 5 * c + 4)
                tmap = tmdt[:, :]
                dt3 = bass.AP(tmap.tensor, tmap.offset + c * 64,
                              [list(tmap.ap[0]), [1, 32], [0, 64]])
                dec3 = bass.AP(tmap.tensor, tmap.offset + c * 64 + 32,
                               [list(tmap.ap[0]), [1, 32], [0, 64]])
                v.tensor_tensor(xdt_tm[:, :], x_tm[:, :], dt3, ALU.mult)
                v.tensor_tensor(xdtd_t[:, :], xdt_tm[:, :], dec3, ALU.mult)
                sm.inc(v.drain(), "g9", 1)
                cbt_b = bass.AP(cbt.tensor, cbt.offset, [list(cbt.ap[0]), [0, 4], [1, 128]])
                for half in range(2):
                    hh0 = half * 16
                    for g4i in range(4):
                        gg = 8 * c + 4 * half + g4i
                        v.wait_ge(sm.h["g10"], g0["g10"] + gg + 1)
                        if gg >= 4:
                            v.wait_ge(sm.h["g4"], ydiag_rec[gg - 4])
                        sm.inc(v.tensor_tensor(Mh[gg % 4][:, :], expss[gg % 2][:, :], cbt_b,
                                               ALU.mult), "g5", 1)
                    for a in range(2):
                        v.wait_ge(sm.h["g4"], ydiag_rec[8 * c + 4 * half + 2 * a + 1])
                        dst = Y_tm[:, (hh0 + 8 * a) * 64: (hh0 + 8 * a) * 64 + 512]
                        if c == 0:
                            sm.inc(v.tensor_copy(dst, p_yd[a][:, :]), "g7", 1)
                        else:
                            exap = expacs[:, :]
                            ex3 = bass.AP(exap.tensor,
                                          exap.offset + c * 32 + hh0 + 8 * a,
                                          [list(exap.ap[0]), [1, 8], [0, 64]])
                            v.tensor_tensor(dst, p_yo[a][:, :], ex3, ALU.mult)
                            sm.inc(v.tensor_tensor(dst, dst, p_yd[a][:, :], ALU.add),
                                   "g7", 1)
                    sbap = smallb[:, :]
                    d3h = bass.AP(sbap.tensor, sbap.offset + CB["dtm32"] + hh0,
                                  [list(sbap.ap[0]), [1, 16], [0, 64]])
                    v.tensor_tensor(tmp_tm[:, 0:1024], x_tm[:, hh0 * 64: hh0 * 64 + 1024],
                                    d3h, ALU.mult)
                    sm.inc(v.tensor_tensor(Y_tm[:, hh0 * 64: hh0 * 64 + 1024],
                                           Y_tm[:, hh0 * 64: hh0 * 64 + 1024],
                                           tmp_tm[:, 0:1024], ALU.add), "g7", 1)
                    for a in range(2):
                        ghg8 = 4 * c + 2 * half + a
                        v.wait_ge(sm.h["g13"], g0["g13"] + ghg8 + 1)
                        v.wait_ge(sm.h["g12"], g0["g12"] + ghg8 + 1)
                        ssl = S_sb[:, (hh0 + 8 * a) * 64: (hh0 + 8 * a) * 64 + 512]
                        v.tensor_tensor(ssl, ssl, p_dr, ALU.mult)
                        sm.inc(v.tensor_tensor(ssl, ssl, p_st, ALU.add), "g6", 1)
                for b in range(4):
                    n = 9 * c + 5 + b
                    v.wait_ge(sm.h["g8"], g0["g8"] + 4 * c + b + 1)
                    bank, cb0 = slotof(n)
                    xbap = xbc[:, :]
                    dst3 = bass.AP(xbap.tensor, xbap.offset + (4 * b) * T + c * 128,
                                   [list(xbap.ap[0]), [T, 4], [1, 128]])
                    sm.inc(v.tensor_copy(dst3, bank[:, cb0:cb0 + 512]), "g2", 1)

        @blk.scalar
        def _(s):
            s = EG(s)
            for c in range(NCH):
                for k in range(5):
                    n = 9 * c + k
                    s.wait_ge(sm.h["g1"], g0["g1"] + 6 * c + k + 1)
                    bank, cb0 = slotof(n)
                    if k < 4:
                        sm.inc(s.activation(x_tm[:, k * 512:(k + 1) * 512],
                                            bank[:, cb0:cb0 + 512], AF.Copy), "g2s", 1)
                    else:
                        sm.inc(s.activation(Br_tm[:, :], bank[:, cb0:cb0 + 128],
                                            AF.Copy), "g2s", 1)
                for g in range(8):
                    half, hg4 = divmod(g, 4)
                    gidx = 8 * c + g
                    s.wait_ge(sm.h["g3"], g0["g3"] + gidx + 1)
                    if gidx >= 2:
                        s.wait_ge(sm.h["g5"], g0["g5"] + gidx - 1)
                    for hi in range(4):
                        h = half * 16 + hg4 * 4 + hi
                        s.activation(expss[gidx % 2][:, hi * 128:(hi + 1) * 128],
                                     p_ss[:, hi * 128:(hi + 1) * 128], AF.Exp,
                                     bias=negacs[:, c * 32 + h:c * 32 + h + 1])
                    sm.inc(s.drain(), "g10", 1)

    pst = ExitStack()
    blk = nc.Block()
    blk.__enter__()
    if True:
        p_h = [pst.enter_context(nc.psum_tensor(f"p_h{i}", [128, 512], F32)) for i in range(8)]
        h1s0 = sm.snap("h1s"); h3s0 = sm.snap("h3s"); h4s0 = sm.snap("h4s"); h5s0 = sm.snap("h5s")
        wA0 = sm.snap("wA"); wB0 = sm.snap("wB")
        qmmH = sm.snap("qmm"); sqmH = sm.snap("sqm"); sqvH = sm.snap("sqv")
        rowsH = sm.snap("rows"); rowvH = sm.snap("rowv"); zmmH = sm.snap("zmm")
        hrow = xbc[0:1, 32768:34816]
        rstd3_row = xbc[0:1, 32768:34816]
        zscr = xbc[:, 34816:35328]
        sqscr = xbc[:, 35328:35840]
        sqscr2 = warena[:, 4096:4608]
        rowf3 = xbc[:, 35840:36864].bitcast(F32)
        r_mu3 = rowf3[0:1, 0:512]
        r_var3 = rowf3[32:33, 0:512]

        @blk.sync
        def _(sy):
            sm.inc(sy.dma_start(hrow, d_rows), "stm", 16)

        @blk.gpsimd
        def _(g):
            for gi in range(8):
                if gi >= 2:
                    g.wait_ge(sm.h["zmm"], zmmH + 8 * (gi - 1))
                par = gi % 2
                dst = warena[:, par * 2048:(par + 1) * 2048]
                srcap = bass.AP(i_wzzT.tensor, gi * 256,
                                [[DI, 128], [128 * DI, NK], [1, 256]])
                sm.inc(g.dma_start(dst.rearrange("p (k c) -> p k c", k=NK), srcap),
                       "wA" if par == 0 else "wB", 16)

        @blk.tensor
        def _(t):
            t.wait_ge(sm.h["stm"], sm.c["stm"])
            for tt in range(4):
                if tt >= 2:
                    t.wait_ge(sm.h["h1s"], h1s0 + tt - 1)
                sm.inc(t.matmul(p_h[6 + tt % 2][:, :], onesb[:, :],
                                hrow[:, tt * 512:(tt + 1) * 512], start=True, stop=True),
                       "qmm", 1)
            nz = 0
            for gi in range(8):
                par = gi % 2
                t.wait_ge(sm.h["wA" if par == 0 else "wB"],
                          (wA0 if par == 0 else wB0) + 16 * (gi // 2 + 1))
                t.wait_ge(sm.h["h1s"], h1s0 + 4)
                for jj in range(2):
                    for tt in range(4):
                        nz += 1
                        if nz > 4:
                            t.wait_ge(sm.h["h3s"], h3s0 + nz - 4)
                        bank = p_h[(nz - 1) % 4]
                        for k in range(NK):
                            mm = t.matmul(bank[:, :],
                                          warena[:, par * 2048 + k * 256 + jj * 128:
                                                 par * 2048 + k * 256 + jj * 128 + 128],
                                          hbuf[:, k * T + tt * 512: k * T + (tt + 1) * 512],
                                          start=(k == 0), stop=(k == NK - 1))
                        sm.inc(mm, "zmm", 1)
            idx = 0
            for tt in range(4):
                if tt >= 2:
                    t.wait_ge(sm.h["rowv"], rowvH + 2 * (tt - 2) + 1)
                for j in range(16):
                    t.wait_ge(sm.h["sqv"], sqvH + idx + 1)
                    mm = t.matmul(p_h[4 + tt % 2][0:1, :], onescol[:, :],
                                  sqscr if idx % 2 == 0 else sqscr2,
                                  start=(j == 0), stop=(j == 15))
                    sm.inc(mm, "sqm", 1)
                    idx += 1

        @blk.vector
        def _(v):
            v = EG(v)
            v.wait_ge(sm.h["stm"], sm.c["stm"])
            for tt in range(4):
                v.wait_ge(sm.h["qmm"], qmmH + tt + 1)
                for k in range(NK):
                    v.scalar_tensor_tensor(hbuf[:, k * T + tt * 512: k * T + (tt + 1) * 512],
                                           qb[:, k * T + tt * 512: k * T + (tt + 1) * 512],
                                           c_("npw", k), p_h[6 + tt % 2][:, :],
                                           ALU.mult, ALU.mult)
                sm.inc(v.drain(), "h1s", 1)
            nz = 0
            for gi in range(8):
                for jj in range(2):
                    j = gi * 2 + jj
                    for tt in range(4):
                        nz += 1
                        v.wait_ge(sm.h["h3s"], h3s0 + nz)
                        ysl = xbc[:, j * T + tt * 512: j * T + (tt + 1) * 512]
                        sm.inc(v.tensor_tensor(ysl, ysl, zscr, ALU.mult), "h4s", 1)
            idx = 0
            for tt in range(4):
                for j in range(16):
                    if idx >= 2:
                        v.wait_ge(sm.h["sqm"], sqmH + idx - 1)
                    ysl = xbc[:, j * T + tt * 512: j * T + (tt + 1) * 512]
                    sm.inc(v.tensor_mul(sqscr if idx % 2 == 0 else sqscr2, ysl, ysl),
                           "sqv", 1)
                    idx += 1
                v.wait_ge(sm.h["sqm"], sqmH + 16 * (tt + 1))
                v.tensor_scalar(r_mu3, p_h[4 + tt % 2][0:1, :], 1.0 / DI, EPS,
                                ALU.mult, ALU.add)
                sm.inc(v.drain(), "rowv", 1)
                v.wait_ge(sm.h["rows"], rowsH + tt + 1)
                sm.inc(v.tensor_copy(rstd3_row[:, tt * 512:(tt + 1) * 512], r_var3),
                       "rowv", 1)

        @blk.scalar
        def _(s):
            s = EG(s)
            for nz in range(1, 65):
                s.wait_ge(sm.h["zmm"], zmmH + nz)
                if nz >= 2:
                    s.wait_ge(sm.h["h4s"], h4s0 + nz - 1)
                bank = p_h[(nz - 1) % 4]
                sm.inc(s.activation(zscr, bank[:, :], AF.Silu), "h3s", 1)
            for tt in range(4):
                s.wait_ge(sm.h["rowv"], rowvH + 2 * tt + 1)
                s.activation(r_var3, r_mu3, AF.Ln)
                sm.inc(s.activation(r_var3, r_var3, AF.Exp, scale=-0.5), "rows", 1)

    if True:
        p_w = p_h[0:4]
        p_rI = p_h[4:8]
        wA1 = sm.snap("wA"); wB1 = sm.snap("wB"); wS0 = sm.snap("wS")
        i1s0 = sm.snap("h1s"); i2s0 = sm.snap("h2s"); i3s0 = sm.snap("h3s")
        zmmI = sm.snap("zmm"); qmmI = sm.snap("qmm"); rowvI = sm.snap("rowv")
        oscrA = [xbc[:, 34816 + i * 512: 34816 + (i + 1) * 512] for i in range(2)]
        oscrB = [xbc[:, 35840 + i * 512: 35840 + (i + 1) * 512] for i in range(2)]
        sfap = smallf[:, :]
        ngw3 = bass.AP(sfap.tensor, sfap.offset + COL["ngw"],
                       [list(sfap.ap[0]), [1, 16], [0, 256]])

        @blk.gpsimd
        def _(g):
            for gi in range(4):
                if gi < 2:
                    g.wait_ge(sm.h["zmm"], zmmI)
                else:
                    g.wait_ge(sm.h["zmm"], zmmI + 8 * (gi - 1))
                par = gi % 2
                dstb = warena[:, 0:4096] if par == 0 else hbuf[:, 0:4096]
                srcap = bass.AP(i_woutT.tensor, gi * 256,
                                [[D, 128], [128 * D, 16], [1, 256]])
                sm.inc(g.dma_start(dstb.rearrange("p (k c) -> p k c", k=16), srcap),
                       "wA" if par == 0 else "wB", 16)
            for k in range(NK):
                sm.inc(g.dma_start(wg_sb[:, k * 8:(k + 1) * 8],
                                   i_wgate[k * 128:(k + 1) * 128, :]), "cD", 16)

        @blk.sync
        def _(sy):
            sm.inc(sy.dma_start(rowtf[0:1, 0:8], i_bgate), "gtD", 16)
            sm.inc(sy.dma_start(rowtf[0:1, 8:9], i_alpha), "gtD", 16)
            sm.inc(sy.dma_start(c_("b1m", 0, 32), i_b1m), "gtD", 16)
            sm.inc(sy.dma_start(c_("b2m", 0, 8), i_b2m), "gtD", 16)

        @blk.tensor
        def _(t):
            for tt in range(4):
                t.wait_ge(sm.h["rowv"], rowvI - 8 + (5, 7, 6, 8)[tt])
                sm.inc(t.matmul(p_rI[tt][:, :], onesb[:, :],
                                rstd3_row[:, tt * 512:(tt + 1) * 512], start=True, stop=True),
                       "qmm", 1)
            nz = 0
            for gi in range(4):
                par = gi % 2
                t.wait_ge(sm.h["wS"], wS0 + gi + 1)
                wbuf = warena if par == 0 else hbuf
                for jj in range(2):
                    for tt in range(4):
                        nz += 1
                        if nz == 1:
                            t.wait_ge(sm.h["h3s"], i3s0)
                        if nz > 4:
                            t.wait_ge(sm.h["h1s"], i1s0 + nz - 4)
                        bank = p_w[(nz - 1) % 4]
                        for k in range(16):
                            mm = t.matmul(bank[:, :],
                                          wbuf[:, k * 256 + jj * 128: k * 256 + jj * 128 + 128],
                                          xbc[:, k * T + tt * 512: k * T + (tt + 1) * 512],
                                          start=(k == 0), stop=(k == 15))
                        sm.inc(mm, "zmm", 1)

        @blk.vector
        def _(v):
            v = EG(v)
            v.wait_ge(sm.h["h3s"], i3s0)
            def scale_w(gi):
                par = gi % 2
                v.wait_ge(sm.h["wA" if par == 0 else "wB"],
                          (wA1 if par == 0 else wB1) + 16 * (gi // 2 + 1))
                dstb = warena[:, 0:4096] if par == 0 else hbuf[:, 0:4096]
                sm.inc(v.tensor_tensor(dstb, dstb, ngw3, ALU.mult), "wS", 1)
            scale_w(0)
            scale_w(1)
            for nz in range(1, 33):
                j, tt = (nz - 1) // 4, (nz - 1) % 4
                v.wait_ge(sm.h["zmm"], zmmI + nz)
                if nz <= 4:
                    v.wait_ge(sm.h["qmm"], qmmI + tt + 1)
                    v.tensor_copy(S_sb[:, tt * 512:(tt + 1) * 512], p_rI[tt][:, :])
                if nz >= 3:
                    v.wait_ge(sm.h["h2s"], i2s0 + nz - 2)
                bank = p_w[(nz - 1) % 4]
                v.tensor_tensor(oscrA[(nz - 1) % 2], bank[:, :],
                                S_sb[:, tt * 512:(tt + 1) * 512], ALU.mult)
                sm.inc(v.tensor_tensor(oscrA[(nz - 1) % 2], oscrA[(nz - 1) % 2],
                                       qb[:, j * T + tt * 512: j * T + (tt + 1) * 512],
                                       ALU.add), "h1s", 1)
                if nz == 8:
                    scale_w(2)
                if nz == 16:
                    scale_w(3)
                if nz >= 2:
                    pz = nz - 1
                    pjj, ptt2 = (pz - 1) // 4, (pz - 1) % 4
                    v.wait_ge(sm.h["h2s"], i2s0 + pz)
                    xsl = x0b[:, pjj * T + ptt2 * 512: pjj * T + (ptt2 + 1) * 512]
                    sm.inc(v.tensor_tensor(xsl, xsl, oscrB[(pz - 1) % 2], ALU.add), "h3s", 1)
            v.wait_ge(sm.h["h2s"], i2s0 + 32)
            xsl = x0b[:, 7 * T + 3 * 512: 7 * T + 4 * 512]
            sm.inc(v.tensor_tensor(xsl, xsl, oscrB[(32 - 1) % 2], ALU.add), "h3s", 1)

        @blk.scalar
        def _(s):
            s = EG(s)
            for nz in range(1, 33):
                j = (nz - 1) // 4
                s.wait_ge(sm.h["h1s"], i1s0 + nz)
                if nz >= 3:
                    s.wait_ge(sm.h["h3s"], i3s0 + nz - 2)
                sm.inc(s.activation(oscrB[(nz - 1) % 2], oscrA[(nz - 1) % 2],
                                    AF.Copy, scale=c_("ga", j)), "h2s", 1)

    if True:
        jk_banks = p_h
        p_s2a = [jk_banks[4][0:1, :], jk_banks[5][0:1, :]]
        p_q2a = [jk_banks[6][0:1, :], jk_banks[7][0:1, :]]
        p_rJ = jk_banks[0:4]
        sqvJ = sm.snap("sqv"); sqmJ = sm.snap("sqm"); rowvJ = sm.snap("rowv")
        rowsJ = sm.snap("rows"); qmmJ = sm.snap("qmm"); h1sJ = sm.snap("h1s")
        h2sJ = sm.snap("h2s"); gtDJ = sm.snap("gtD"); cDJ = sm.snap("cD")
        snap["Jg"] = (qmmJ, gtDJ, rowsJ)
        mu2_row = xbc[0:1, 32768:34816]
        rs2_row = xbc[0:1, 30720:32768]
        sqscrJ = xbc[:, 35328:35840]
        sqscrJ2 = warena[:, 4096:4608]
        rowfJ = xbc[:, 35840:36864].bitcast(F32)
        r_muJ = rowfJ[0:1, 0:512]
        r_varJ = rowfJ[32:33, 0:512]
        musqJ = xbc[:, 30720:31744].bitcast(F32)[32:33, 0:512]
        applyscr = xbc[:, 34816:35328]

        @blk.sync
        def _(sy):
            sy.wait_ge(sm.h["h2s"], h2sJ + 2)
            with sy.register("re1") as re1, sy.register("re2") as re2, \
                 sy.register("ro") as ro:
                sy.reg_load(re1, idx8[0:1, 0:1])
                sy.reg_load(re2, idx8[0:1, 1:2])
                sy.reg_mul(ro, re1, 4096)
                sm.inc(sy.dma_start(c_("b1e1", 0, 32),
                                    bass.AP(i_b1e.tensor, ro, [[32, 128], [1, 32]])),
                       "gt2", 16)
                sy.reg_mul(ro, re2, 4096)
                sm.inc(sy.dma_start(c_("b1e2", 0, 32),
                                    bass.AP(i_b1e.tensor, ro, [[32, 128], [1, 32]])),
                       "gt2", 16)
                sy.reg_mul(ro, re1, 1024)
                sm.inc(sy.dma_start(c_("b2e1", 0, 8),
                                    bass.AP(i_b2e.tensor, ro, [[8, 128], [1, 8]])),
                       "gt2", 16)
                sy.reg_mul(ro, re2, 1024)
                sm.inc(sy.dma_start(c_("b2e2", 0, 8),
                                    bass.AP(i_b2e.tensor, ro, [[8, 128], [1, 8]])),
                       "gt2", 16)

        @blk.vector
        def _(v):
            v = EG(v)
            idx = 0
            for tt in range(4):
                for k in range(NK):
                    if idx >= 2:
                        v.wait_ge(sm.h["sqm"], sqmJ + idx - 1)
                    xs = x0b[:, k * T + tt * 512: k * T + (tt + 1) * 512]
                    sm.inc(v.tensor_mul(sqscrJ if idx % 2 == 0 else sqscrJ2, xs, xs),
                           "sqv", 1)
                    idx += 1
                v.wait_ge(sm.h["sqm"], sqmJ + 8 * (tt + 1))
                v.tensor_scalar(r_muJ, p_q2a[tt % 2][:, :], 1.0 / D, None, ALU.mult)
                v.tensor_copy(mu2_row[:, tt * 512:(tt + 1) * 512], r_muJ)
                v.tensor_scalar(r_varJ, p_s2a[tt % 2][:, :], 1.0 / D, None, ALU.mult)
                v.drain()
                v.tensor_mul(rowfJ[0:1, 512:1024] if False else r_varJ, r_varJ, r_varJ) if False else None
                v.tensor_mul(musqJ, r_muJ, r_muJ)
                v.tensor_sub(r_varJ, r_varJ, musqJ)
                v.tensor_scalar(r_varJ, r_varJ, 1.0, EPS, ALU.mult, ALU.add)
                sm.inc(v.drain(), "rowv", 1)
                v.wait_ge(sm.h["rows"], rowsJ + tt + 1)
                sm.inc(v.tensor_copy(rs2_row[:, tt * 512:(tt + 1) * 512], r_varJ), "rowv", 1)
            for tt in range(4):
                v.wait_ge(sm.h["qmm"], qmmJ + tt + 1)
                for k in range(NK):
                    xs = x0b[:, k * T + tt * 512: k * T + (tt + 1) * 512]
                    v.tensor_sub(applyscr, xs, p_rJ[2 * (tt % 2)][:, :])
                    v.tensor_mul(applyscr, applyscr, p_rJ[2 * (tt % 2) + 1][:, :])
                    v.tensor_scalar(qb[:, k * T + tt * 512: k * T + (tt + 1) * 512],
                                    applyscr, c_("onepcb", k), c_("mod", 24 + k),
                                    ALU.mult, ALU.add)
                sm.inc(v.drain(), "h1s", 1)
            for k in range(NK):
                v.tensor_reduce(c_("meancol", k), qb[:, k * T:(k + 1) * T], AX, ALU.add)
            v.drain()
            v.tensor_scalar(c_("meancol", 0, NK), c_("meancol", 0, NK), 1.0 / T, None, ALU.mult)
            v.drain()
            v.wait_ge(sm.h["cD"], cDJ)
            sm.inc(v.tensor_copy(cb_("mean8", 0, NK), c_("meancol", 0, NK)), "h1s", 1)

        @blk.scalar
        def _(s):
            s = EG(s)
            for tt in range(4):
                s.wait_ge(sm.h["rowv"], rowvJ + 2 * tt + 1)
                s.activation(r_varJ, r_varJ, AF.Ln)
                sm.inc(s.activation(r_varJ, r_varJ, AF.Exp, scale=-0.5), "rows", 1)

        @blk.tensor
        def _(t):
            idx = 0
            for tt in range(4):
                if tt >= 2:
                    t.wait_ge(sm.h["rowv"], rowvJ + 2 * (tt - 2) + 1)
                for k in range(NK):
                    t.wait_ge(sm.h["sqv"], sqvJ + idx + 1)
                    t.matmul(p_q2a[tt % 2][:, :], onescol[:, :],
                             x0b[:, k * T + tt * 512: k * T + (tt + 1) * 512],
                             start=(k == 0), stop=(k == NK - 1))
                    mm = t.matmul(p_s2a[tt % 2][:, :], onescol[:, :],
                                  sqscrJ if idx % 2 == 0 else sqscrJ2,
                                  start=(k == 0), stop=(k == NK - 1))
                    sm.inc(mm, "sqm", 1)
                    idx += 1
            for tt in range(4):
                t.wait_ge(sm.h["rowv"], rowvJ + 2 * tt + 2)
                if tt >= 2:
                    t.wait_ge(sm.h["h1s"], h1sJ + tt - 1)
                t.matmul(p_rJ[2 * (tt % 2)][:, :], onesb[:, :],
                         mu2_row[:, tt * 512:(tt + 1) * 512], start=True, stop=True)
                sm.inc(t.matmul(p_rJ[2 * (tt % 2) + 1][:, :], onesb[:, :],
                                rs2_row[:, tt * 512:(tt + 1) * 512], start=True, stop=True),
                       "qmm", 1)

    if True:
        p_f = jk_banks
        eA0 = sm.snap("eAD"); eB0 = sm.snap("eBD")
        wA2 = sm.snap("wA"); wB2 = sm.snap("wB")
        f1s0 = sm.snap("h4s")
        f2s0 = sm.snap("h5s")
        f3s0 = sm.snap("h3s")
        f4s0 = sm.snap("h1s")
        zmmK = sm.snap("zmm")
        qmmK = sm.snap("qmm")
        oD0 = sm.snap("oD")
        oE0 = sm.snap("oE")
        h2sJ3 = sm.snap("h2s")
        A2sb = hbuf[:, :].bitcast(F32)
        negbc = negacs[:, :].bitcast(BF16)
        t3slot = negbc[:, 0:512]
        x2slot = [negbc[:, 512:1024], warena[:, 4096:4608]]
        w2B = [S_sb[:, i * 128:(i + 1) * 128] for i in range(16)] + \
              [tmdt[:, i * 128:(i + 1) * 128] for i in range(8)] + \
              [expacs.bitcast(BF16)[:, i * 128:(i + 1) * 128] for i in range(8)]

        def w1buf(par):
            return xbc[:, 32768 + par * 2048: 32768 + (par + 1) * 2048]

        NA1 = 16 * 2 * 2
        NA2 = 8 * 2
        p_lg = p_f[7][0:1, 0:8]
        p_cf = p_f[7][:, 8:12]
        qmmJg, gtDJg, rowsJg = snap["Jg"]

        gwsem = {(0, 1): "gwD", (0, 2): "gw2", (1, 1): "gw3", (1, 2): "gw4"}
        gw0 = {k: sm.snap(v) for k, v in gwsem.items()}
        W1N = 16
        W2N = 64

        @blk.gpsimd
        def _(g):
            with g.register("je") as je, g.register("jo") as jo:
                gathered = False

                def emit_gathers():
                    g.wait_ge(sm.h["h2s"], h2sJ3 + 2)
                    for s in range(2):
                        g.reg_load(je, idx8[0:1, s:s + 1])
                        g.reg_mul(je, je, D * 4096)
                        sm.inc(g.dma_start(
                            bass.AP(d_w1b.tensor, s * D * 4096,
                                    [[4096, 1024], [1, 4096]]),
                            bass.AP(i_w1e.tensor, je, [[4096, 1024], [1, 4096]])),
                               gwsem[(s, 1)], 16)
                        for r in range(4):
                            g.reg_add(jo, je, r * 1024 * D)
                            sm.inc(g.dma_start(
                                bass.AP(d_w2b.tensor, s * 4096 * D + r * 1024 * D,
                                        [[D, 1024], [1, 1024]]),
                                bass.AP(i_w2e.tensor, jo, [[D, 1024], [1, 1024]])),
                                   gwsem[(s, 2)], 16)

                for hf in range(2):
                    if hf == 1:
                        for j2 in range(8):
                            for t2 in range(2):
                                oidx = (0 * 8 + j2) * 2 + t2
                                g.wait_ge(sm.h["h3s"], f3s0 + oidx + 1)
                                sm.inc(g.dma_start(
                                    o_out[j2 * 128:(j2 + 1) * 128,
                                          0 * 1024 + t2 * 512: (t2 + 1) * 512],
                                    x2slot[oidx % 2]), "oD" if oidx % 2 == 0 else "oE", 16)
                    for p in range(3):
                        for grp in range(16):
                            par = grp % 2
                            gidx = (hf * 3 + p) * 16 + grp
                            if gidx == 0:
                                g.wait_ge(sm.h["qmm"], qmmK)
                            if gidx == 1:
                                g.wait_ge(sm.h["h1s"], f4s0 - 1)
                            if gidx == 10 and not gathered:
                                emit_gathers()
                                gathered = True
                            if gidx >= 2:
                                g.wait_ge(sm.h["zmm"], zmmK + 4 * (gidx - 1))
                            dst = w1buf(par).rearrange("p (k c) -> p k c", k=NK)
                            if p == 0:
                                srcap = bass.AP(i_w1m.tensor, grp * 256,
                                                [[4096, 128], [128 * 4096, NK], [1, 256]])
                            else:
                                g.wait_ge(sm.h[gwsem[(p - 1, 1)]],
                                          gw0[(p - 1, 1)] + W1N)
                                srcap = bass.AP(d_w1b.tensor,
                                                (p - 1) * D * 4096 + grp * 256,
                                                [[4096, 128], [128 * 4096, NK], [1, 256]])
                            sm.inc(g.dma_start(dst, srcap),
                                   "eAD" if par == 0 else "eBD", 16)
                        for j2 in range(8):
                            par = j2 % 2
                            sidx = (hf * 3 + p) * 8 + j2
                            if sidx == 0:
                                g.wait_ge(sm.h["zmm"], zmmK)
                            if sidx == 1:
                                g.wait_ge(sm.h["h1s"], f4s0 - 5)
                            if sidx >= 2:
                                g.wait_ge(sm.h["qmm"], qmmK + 4 + 2 * (sidx - 1))
                            if p >= 1:
                                g.wait_ge(sm.h[gwsem[(p - 1, 2)]],
                                          gw0[(p - 1, 2)] + W2N)
                            if par == 0:
                                dst = warena[:, 0:4096].rearrange("p (k c) -> p k c", k=32)
                                if p == 0:
                                    srcap = bass.AP(i_w2m.tensor, j2 * 128,
                                                    [[D, 128], [128 * D, 32], [1, 128]])
                                else:
                                    srcap = bass.AP(d_w2b.tensor,
                                                    (p - 1) * 4096 * D + j2 * 128,
                                                    [[D, 128], [128 * D, 32], [1, 128]])
                                sm.inc(g.dma_start(dst, srcap), "wA", 16)
                            else:
                                for piece, (ks, ke) in enumerate(((0, 16), (16, 24), (24, 32))):
                                    if p == 0:
                                        srcap = bass.AP(i_w2m.tensor, j2 * 128 + ks * 128 * D,
                                                        [[D, 128], [128 * D, ke - ks], [1, 128]])
                                    else:
                                        srcap = bass.AP(d_w2b.tensor,
                                                        (p - 1) * 4096 * D + j2 * 128 + ks * 128 * D,
                                                        [[D, 128], [128 * D, ke - ks], [1, 128]])
                                    if piece == 0:
                                        dstap = S_sb[:, 0:2048].rearrange("p (k c) -> p k c", k=16)
                                    elif piece == 1:
                                        dstap = tmdt[:, 0:1024].rearrange("p (k c) -> p k c", k=8)
                                    else:
                                        dstap = expacs[:, :].bitcast(BF16).rearrange("p (k c) -> p k c", k=8)
                                    sm.inc(g.dma_start(dstap, srcap), "wB", 16)
                for j2 in range(8):
                    for t2 in range(2):
                        oidx = (1 * 8 + j2) * 2 + t2
                        g.wait_ge(sm.h["h3s"], f3s0 + oidx + 1)
                        sm.inc(g.dma_start(
                            o_out[j2 * 128:(j2 + 1) * 128,
                                  1024 + t2 * 512: 1024 + (t2 + 1) * 512],
                            x2slot[oidx % 2]), "oD" if oidx % 2 == 0 else "oE", 16)

        @blk.tensor
        def _(t):
            na1 = 0
            na2 = 0
            for hf in range(2):
                for p in range(3):
                    pidx = hf * 3 + p
                    for grp in range(16):
                        par = grp % 2
                        gidx = pidx * 16 + grp
                        if gidx == 5:
                            t.wait_ge(sm.h["h1s"], f4s0)
                            t.wait_ge(sm.h["cD"], sm.c["cD"])
                            for k in range(NK):
                                mm = t.matmul(p_lg, cb_("mean8", k, 1),
                                              wg_sb[:, k * 8:(k + 1) * 8],
                                              start=(k == 0), stop=(k == NK - 1))
                            sm.inc(mm, "qmm", 1)
                        if gidx == 10:
                            t.wait_ge(sm.h["h2s"], h2sJ3 + 2)
                            for i in range(3):
                                sm.inc(t.matmul(p_cf[:, i:i + 1], onesb[:, :],
                                                rowtb[0:1, i:i + 1],
                                                start=True, stop=True), "qmm", 1)
                        t.wait_ge(sm.h["eAD" if par == 0 else "eBD"],
                                  (eA0 if par == 0 else eB0) + 16 * (gidx // 2 + 1))
                        for jj in range(2):
                            j = grp * 2 + jj
                            for t2 in range(2):
                                na1 += 1
                                if na1 == 1:
                                    t.wait_ge(sm.h["h1s"], f4s0 - 1)
                                if na1 > 4:
                                    t.wait_ge(sm.h["h4s"], f1s0 + na1 - 4)
                                bank = p_f[(na1 - 1) % 4]
                                for k in range(NK):
                                    mm = t.matmul(bank[:, :],
                                                  w1buf(par)[:, k * 256 + jj * 128: k * 256 + jj * 128 + 128],
                                                  qb[:, k * T + hf * 1024 + t2 * 512:
                                                     k * T + hf * 1024 + (t2 + 1) * 512],
                                                  start=(k == 0), stop=(k == NK - 1))
                                sm.inc(mm, "zmm", 1)
                    for j2 in range(8):
                        par = j2 % 2
                        sidx = pidx * 8 + j2
                        t.wait_ge(sm.h["wA" if par == 0 else "wB"],
                                  (wA2 if par == 0 else wB2) +
                                  (16 * (sidx // 2 + 1) if par == 0 else 48 * (sidx // 2 + 1)))
                        for t2 in range(2):
                            na2 += 1
                            if na2 == 4:
                                t.wait_ge(sm.h["h2s"], h2sJ3 + 3)
                            if na2 > 4:
                                t.wait_ge(sm.h["h5s"], f2s0 + na2 - 4)
                            t.wait_ge(sm.h["h4s"], f1s0 + pidx * 64 + 64)
                            if p == 0:
                                t.wait_ge(sm.h["h1s"], f4s0 + hf * 8 + j2 + 1)
                            bank = p_f[4 + (na2 - 1) % 4]
                            for k in range(32):
                                wtile = (warena[:, k * 128:(k + 1) * 128] if par == 0
                                         else w2B[k])
                                mm = t.matmul(bank[:, :], wtile,
                                              xbc[:, k * 1024 + t2 * 512: k * 1024 + (t2 + 1) * 512],
                                              start=(k == 0), stop=(k == 31))
                            sm.inc(mm, "qmm", 1)

        @blk.scalar
        def _(s):
            s = EG(s)
            s.wait_ge(sm.h["gtD"], sm.c["gtD"])
            na1 = 0
            expert_bias_ok = False
            for hf in range(2):
                for p in range(3):
                    if p >= 1 and not expert_bias_ok:
                        s.wait_ge(sm.h["gt2"], sm.c["gt2"])
                        expert_bias_ok = True
                    bcol = {0: "b1m", 1: "b1e1", 2: "b1e2"}[p]
                    for grp in range(16):
                        for jj in range(2):
                            j = grp * 2 + jj
                            for t2 in range(2):
                                na1 += 1
                                if na1 == 20:
                                    s.wait_ge(sm.h["h2s"], h2sJ3 + 1)
                                    sm.inc(s.activation(rowtf[0:1, 25:26],
                                                        rowtf[0:1, 24:25], AF.Exp,
                                                        scale=-1.0), "rows", 1)
                                s.wait_ge(sm.h["zmm"], zmmK + na1)
                                bank = p_f[(na1 - 1) % 4]
                                sm.inc(s.activation(
                                    xbc[:, j * 1024 + t2 * 512: j * 1024 + (t2 + 1) * 512],
                                    bank[:, :], AF.Gelu_apprx_tanh,
                                    bias=c_(bcol, j)), "h4s", 1)
                for j2 in range(8):
                    for t2 in range(2):
                        oidx = (hf * 8 + j2) * 2 + t2
                        s.wait_ge(sm.h["h5s"], f2s0 + (hf * 3 + 2) * 16 + (j2 * 2 + t2) + 1)
                        if oidx >= 1:
                            s.wait_ge(sm.h["h3s"], f3s0 + oidx)
                        sm.inc(s.activation(t3slot,
                                            A2sb[:, j2 * 1024 + t2 * 512: j2 * 1024 + (t2 + 1) * 512],
                                            AF.Copy, scale=c_("gb", j2)), "h2s", 1)

        @blk.vector
        def _(v):
            v = EG(v)
            v.wait_ge(sm.h["qmm"], qmmJg + 5)
            v.wait_ge(sm.h["gtD"], gtDJg)
            lgrow = rowtf[0:1, 16:24]
            v.tensor_tensor(lgrow, p_lg, rowtf[0:1, 0:8], ALU.add)
            v.drain()
            v.max(top8[:, :], lgrow)
            v.drain()
            v.max_index(idx8[:, :], top8[:, :], lgrow)
            v.drain()
            d12 = rowtf[0:1, 24:25]
            v.tensor_sub(d12, top8[0:1, 0:1], top8[0:1, 1:2])
            sm.inc(v.drain(), "h2s", 1)
            v.wait_ge(sm.h["rows"], rowsJg + 5)
            g1 = rowtf[0:1, 25:26]
            v.tensor_scalar(g1, g1, 1.0, None, ALU.add)
            v.reciprocal(g1, g1)
            g2 = rowtf[0:1, 26:27]
            ac = rowtf[0:1, 27:28]
            amc = rowtf[0:1, 28:29]
            ce1 = rowtf[0:1, 29:30]
            ce2 = rowtf[0:1, 30:31]
            v.tensor_scalar(g2, g1, -1.0, 1.0, ALU.mult, ALU.add)
            v.tensor_scalar(ac, rowtf[0:1, 8:9], 0.1, 1.0, ALU.max, ALU.min)
            v.tensor_scalar(amc, ac, -1.0, 1.0, ALU.mult, ALU.add)
            v.tensor_mul(ce1, amc, g1)
            v.tensor_mul(ce2, amc, g2)
            v.tensor_copy(rowtb[0:1, 0:1], ac)
            v.tensor_copy(rowtb[0:1, 1:2], ce1)
            v.tensor_copy(rowtb[0:1, 2:3], ce2)
            sm.inc(v.drain(), "h2s", 1)
            v.wait_ge(sm.h["qmm"], qmmJg + 8)
            v.tensor_copy(c_("coefm"), p_cf[:, 0:1])
            v.tensor_copy(c_("coefe1"), p_cf[:, 1:2])
            v.tensor_copy(c_("coefe2"), p_cf[:, 2:3])
            v.drain()
            v.wait_ge(sm.h["gtD"], sm.c["gtD"])
            v.wait_ge(sm.h["gt2"], sm.c["gt2"])
            v.tensor_scalar(c_("bcomb", 0, 8), c_("b2m", 0, 8), c_("coefm"), None, ALU.mult)
            v.drain()
            v.scalar_tensor_tensor(c_("bcomb", 0, 8), c_("b2e1", 0, 8),
                                   c_("coefe1"), c_("bcomb", 0, 8), ALU.mult, ALU.add)
            v.drain()
            sm.inc(v.scalar_tensor_tensor(c_("bcomb", 0, 8), c_("b2e2", 0, 8),
                                          c_("coefe2"), c_("bcomb", 0, 8),
                                          ALU.mult, ALU.add), "h2s", 1)
            na2 = 0
            for hf in range(2):
                v.wait_ge(sm.h["h2s"], h2sJ3 + 3)
                for j2 in range(8):
                    if hf == 1:
                        v.wait_ge(sm.h["h2s"], h2sJ3 + 3 + 16 * hf)
                    sm.inc(v.tensor_copy(A2sb[:, j2 * 1024:(j2 + 1) * 1024],
                                         c_("bcomb", j2).broadcast_to([128, 1024])),
                           "h1s", 1)
                for p in range(3):
                    pidx = hf * 3 + p
                    coefn = {0: "coefm", 1: "coefe1", 2: "coefe2"}[p]
                    for j2 in range(8):
                        for t2 in range(2):
                            na2 += 1
                            v.wait_ge(sm.h["qmm"], qmmK + 4 + na2)
                            bank = p_f[4 + (na2 - 1) % 4]
                            asl = A2sb[:, j2 * 1024 + t2 * 512: j2 * 1024 + (t2 + 1) * 512]
                            sm.inc(v.scalar_tensor_tensor(asl, bank[:, :], c_(coefn),
                                                          asl, ALU.mult, ALU.add), "h5s", 1)
                for j2 in range(8):
                    for t2 in range(2):
                        oidx = (hf * 8 + j2) * 2 + t2
                        v.wait_ge(sm.h["h2s"], h2sJ3 + 3 + oidx + 1)
                        if oidx >= 2:
                            v.wait_ge(sm.h["oD" if oidx % 2 == 0 else "oE"],
                                      (oD0 if oidx % 2 == 0 else oE0) + 16 * ((oidx - 2) // 2 + 1))
                        sm.inc(v.tensor_tensor(x2slot[oidx % 2],
                                               x0b[:, j2 * T + hf * 1024 + t2 * 512:
                                                   j2 * T + hf * 1024 + (t2 + 1) * 512],
                                               t3slot, ALU.add), "h3s", 1)

    pst.close()
    blk.__exit__(None, None, None)

    with nc.Block() as blk:
        @blk.sync
        def _(sy):
            sy.wait_ge(sm.h["oD"], sm.c["oD"])
            sy.wait_ge(sm.h["oE"], sm.c["oE"])
            sy.wait_ge(sm.h["st"], sm.c["st"])

    with nc.Block() as blk:
        @blk.sync
        def _(sy):
            if db_mod is not None:
                sm.inc(sy.dma_start(db_mod, smallf[:, COL["mod"]:COL["mod"] + 48]), "st", 16)
            if db_q is not None:
                sm.inc(sy.dma_start(db_q, qb[:, :]), "st", 16)
            if db_hin is not None:
                sm.inc(sy.dma_start(db_hin, hbuf[:, :]), "st", 16)
            if db_cos is not None:
                sm.inc(sy.dma_start(db_cos, cosrep[:, :]), "st", 16)
            if db_sin is not None:
                sm.inc(sy.dma_start(db_sin, sinrep2[:, :]), "st", 16)
            if db_xbcc is not None:
                sm.inc(sy.dma_start(db_xbcc, xbc[:, :]), "st", 16)
            if db_dt is not None:
                sm.inc(sy.dma_start(db_dt, dt_fm), "st", 16)
            if db_acs is not None:
                sm.inc(sy.dma_start(db_acs, acs_fm), "st", 16)
            if db_tm is not None:
                sm.inc(sy.dma_start(db_tm, tmdt[:, :]), "st", 16)
            if db_y is not None:
                sm.inc(sy.dma_start(db_y, xbc[:, 0:16 * T]), "st", 16)
            if db_x1 is not None:
                sm.inc(sy.dma_start(db_x1, x0b[:, :]), "st", 16)
            sy.wait_ge(sm.h["st"], sm.snap("st"))

    ctx.close()
    return nc, dbg_outs


def stage_core_inputs(inputs, b):
    A = lambda x: np.ascontiguousarray(x, dtype=np.float32)
    d = {}
    d["xT"] = A(inputs["L0_attn_x"][b].T)
    d["wqT"] = A(inputs["L0_attn_W_in"][:D].T)
    d["npw"] = A(inputs["L0_attn_norm_pre_w"].reshape(NK, 128).T)
    d["ccol"] = A(inputs["L0_c"][b].reshape(NK, 128).T)
    d["wmod"] = A(inputs["L0_wmod"])
    d["identf"] = np.eye(128, dtype=np.float32)
    d["wzxcT"] = A(inputs["L0_attn_W_zxbcdt"][DI:DI + DTW].T)
    d["convw"] = A(inputs["L0_attn_conv_w"].reshape(NJC, 128, 4).transpose(1, 0, 2).reshape(128, NJC * 4))
    d["convb"] = A(inputs["L0_attn_conv_b"].reshape(NJC, 128).T)
    d["dtb"] = A(inputs["L0_attn_dt_bias"].reshape(H, 1))
    d["alog"] = A(inputs["L0_attn_A_log"].reshape(H, 1))
    d["trilneg"] = np.tril(np.full((128, 128), -30000.0, dtype=np.float32), -1)
    d["drep"] = A(np.repeat(inputs["L0_attn_D_param"], 64).reshape(16, 128).T)
    d["dtm32"] = A(np.broadcast_to(np.asarray(inputs["L0_attn_D_param"]).reshape(1, 32), (128, 32)))
    d["wzzT"] = A(inputs["L0_attn_W_zxbcdt"][0:DI].T)
    d["woutT"] = A(inputs["L0_attn_W_out"].T)
    d["ngw"] = A(inputs["L0_attn_norm_gated_w"].reshape(16, 128).T)
    d["wgate"] = A(inputs["L0_ffn_w_gate"])
    d["bgate"] = A(inputs["L0_ffn_b_gate"].reshape(1, 8))
    d["alpha"] = A(np.asarray(inputs["L0_ffn_alpha"]).reshape(1, 1))
    d["w1m"] = A(inputs["L0_ffn_W1m"])
    d["b1m"] = A(inputs["L0_ffn_b1m"].reshape(32, 128).T)
    d["w2m"] = A(inputs["L0_ffn_W2m"])
    d["b2m"] = A(inputs["L0_ffn_b2m"].reshape(8, 128).T)
    d["w1e"] = A(inputs["L0_ffn_W1e"].reshape(8 * D, 4096))
    d["b1e"] = A(inputs["L0_ffn_b1e"].reshape(8, 32, 128).transpose(0, 2, 1).reshape(8 * 128, 32))
    d["w2e"] = A(inputs["L0_ffn_W2e"].reshape(8 * 4096, D))
    d["b2e"] = A(inputs["L0_ffn_b2e"].reshape(8, 8, 128).transpose(0, 2, 1).reshape(8 * 128, 8))
    d["iota32"] = np.arange(H, dtype=np.float32).reshape(H, 1)
    return d



def kernel(**inputs):
    inputs = {k: np.asarray(v) for k, v in inputs.items()}
    nc, _ = build_core_program(())
    in_maps = [stage_core_inputs(inputs, b) for b in range(8)]
    res = run_bass_kernel_spmd(nc, in_maps, core_ids=list(range(8)))
    out = np.stack([np.ascontiguousarray(res.results[b]["o_xT"].T) for b in range(8)], axis=0)
    return out.astype(np.float32)
